# version 19
# baseline (speedup 1.0000x reference)
import contextlib
import numpy as np
import ml_dtypes
import concourse.bass as bass
import concourse.mybir as mybir
from concourse.bass_utils import run_bass_kernel_spmd

F32 = mybir.dt.float32
BF16 = mybir.dt.bfloat16
AF = mybir.ActivationFunctionType
ALU = mybir.AluOpType
AX = mybir.AxisListType

D = 1024
KT = 8
DFF = 4096
NE = 8
OWN = 4096
HALO = 256
TOT = OWN + HALO
NT_MAX = 10
CW = NT_MAX * 128
LN_EPS = 1e-5
INV_SCALE = 1.0 / 512.0
NDM = 16
W1BLK = 512
NBLK = 2048 // W1BLK
BDM = W1BLK // 128
V_OFF = KT * CW

MASK_MU, MASK_ML, MASK_ONES, MASK_A, MASK_B, MASK_C = range(6)

CAP = 256
NST = CAP // 128

CHUNKS = [(10, 2), (8, 0), (8, 0), (8, 0)]


def build_program(n_chunks=4, n_experts=NE, do_attn=True, do_ffn2=True):
    nc = bass.Bass("TRN2", target_bir_lowering=False)

    x_in = nc.dram_tensor("x_in", [TOT, D], F32, kind="ExternalInput").ap()
    w1 = nc.dram_tensor("w1", [NE, D, DFF], F32, kind="ExternalInput").ap()
    w2 = nc.dram_tensor("w2", [NE, DFF, D], F32, kind="ExternalInput").ap()
    wqkv = nc.dram_tensor("wqkv", [D, 3 * D], F32, kind="ExternalInput").ap()
    wr = nc.dram_tensor("wr", [D, NE], F32, kind="ExternalInput").ap()
    br = nc.dram_tensor("br", [NE], F32, kind="ExternalInput").ap()
    b1 = nc.dram_tensor("b1", [NE, DFF], F32, kind="ExternalInput").ap()
    b2 = nc.dram_tensor("b2", [NE, D], F32, kind="ExternalInput").ap()
    bqkv = nc.dram_tensor("bqkv", [3 * D], F32, kind="ExternalInput").ap()
    lnw = nc.dram_tensor("lnw", [D], F32, kind="ExternalInput").ap()
    lnb = nc.dram_tensor("lnb", [D], F32, kind="ExternalInput").ap()
    idf = nc.dram_tensor("idf", [128, 128], F32, kind="ExternalInput").ap()
    masks = nc.dram_tensor("masks", [5, 128, 128], F32, kind="ExternalInput").ap()
    ones_in = nc.dram_tensor("ones_in", [1, 512], F32, kind="ExternalInput").ap()
    tri = nc.dram_tensor("tri", [128, 136], BF16, kind="ExternalInput").ap()
    ecap = nc.dram_tensor("ecap", [8, 2], F32, kind="ExternalInput").ap()
    out = nc.dram_tensor("out", [OWN, D], F32, kind="ExternalOutput").ap()
    w1s = nc.dram_tensor("w1s", [NE, 16 // BDM * 2, 128, KT * W1BLK], BF16, kind="Internal").ap()
    w2s = nc.dram_tensor("w2s", [NE, 2, 128, NDM * 1024], BF16, kind="Internal").ap()
    wqs = nc.dram_tensor("wqs", [6, 128, KT * 512], BF16, kind="Internal").ap()
    xg = nc.dram_tensor("xg", [NE * 512, D], BF16, kind="Internal").ap()
    yg = nc.dram_tensor("yg", [NE * 512, D], BF16, kind="Internal").ap()
    kvt = nc.dram_tensor("kvt", [128, KT * 256 + 2 * D], BF16, kind="Internal").ap()

    st = contextlib.ExitStack()
    with st:
        sbt = lambda name, shape, dt: st.enter_context(nc.sbuf_tensor(name, shape, dt))
        x_sb = sbt("x_sb", [128, NT_MAX * D], F32)
        xnT = sbt("xnT", [128, KT * CW], BF16)
        bigA = sbt("bigA", [128, NDM * CW], BF16)
        yacc = sbt("yacc", [128, NT_MAX * D], F32)
        w2h = sbt("w2h", [128, NDM * D], BF16)
        w1blk = sbt("w1blk", [128, 2 * KT * W1BLK], BF16)
        xn_bf = sbt("xn_bf", [128, D], F32)
        b1_sb = sbt("b1_sb", [128, NE * 32], F32)
        b2_sb = sbt("b2_sb", [NE, D], BF16)
        bqkvT = sbt("bqkvT", [128, 16], F32)
        wr_sb = sbt("wr_sb", [128, KT * NE], BF16)
        br_row = sbt("br_row", [1, NE], BF16)
        ones_bf = sbt("ones_bf", [1, 128], BF16)
        lnwb = sbt("lnwb", [128, D], F32)
        lnbb = sbt("lnbb", [128, D], F32)
        id_f = sbt("id_f", [128, 128], F32)
        mk_sb = sbt("mk_sb", [128, 5 * 128], BF16)
        S_sb = sbt("S_sb", [128, NT_MAX * NE], F32)
        mskT = sbt("mskT", [8, 2 * 128], BF16)
        stats = sbt("stats", [128, 68], F32)
        tri_sb = sbt("tri_sb", [128, 136], BF16)

        Mbf = sbt("Mbf", [128, 16], BF16)
        bcol = sbt("bcol", [8, 8], F32)
        dint = sbt("dint", [128, 12], mybir.dt.uint32)
        prefs = sbt("prefs", [8, 256], F32).ap()
        Ssel_sb = sbt("Ssel_sb", [128, 12], F32)
        ptk = sbt("ptk", [128, 16], F32)

        P = [st.enter_context(nc.psum_tensor(f"P{i}", [128, 1024], F32)) for i in range(4)]

        sem_names = ["sW1a", "sW1b", "sW2", "sQ", "sX", "sCg", "sCs", "sOut",
                     "sT", "sV", "sA", "sWBa", "sWBb", "sWB2", "sWBq",
                     "sSC", "sYGa", "sYGb", "sGB", "sXG", "sW2h", "sW2g",
                     "sW2i", "sW2j", "sSC2", "sKV", "sGB2", "sW1c", "sW1d"]
        SEM = {n: st.enter_context(nc.semaphore(n)) for n in sem_names}

        ops = {"tensor": [], "scalar": [], "vector": [], "gpsimd": [], "sync": []}
        cnt = {n: 0 for n in sem_names}
        last_wait = {}

        def wait(eng, sem, val):
            if val <= 0:
                return
            k = (eng, sem)
            if last_wait.get(k, 0) >= val:
                return
            last_wait[k] = val
            ops[eng].append(lambda E, s=sem, v=val: E.wait_ge(SEM[s], v))

        def inc(sem, n=1):
            cnt[sem] += n
            return cnt[sem]

        def op(eng, fn, sem=None, n=1):
            if sem is None:
                ops[eng].append(fn)
                return None
            v = inc(sem, n)
            ops[eng].append(lambda E, f=fn, s=sem, k=n: f(E).then_inc(SEM[s], k))
            return v

        def drain(eng):
            ops[eng].append(lambda E: E.drain())

        def barrier(consumers=("tensor", "scalar", "vector")):
            for eng in consumers:
                for s in ("sT", "sV", "sA"):
                    wait(eng, s, cnt[s])

        bank_free = [("sV", 0)] * 8

        def bank_wait(eng, banks):
            for b in banks:
                s, v = bank_free[b]
                wait(eng, s, v)

        def bank_release(banks, sem, val):
            for b in banks:
                bank_free[b] = (sem, val)

        x_t = lambda i: x_sb[:, i * D:(i + 1) * D]
        yacc_t = lambda i: yacc[:, i * D:(i + 1) * D]
        S_t = lambda i: S_sb[:, i * NE:(i + 1) * NE]
        xnT_k = lambda k, a, b: xnT[:, k * CW + a: k * CW + b]
        h1T_t = lambda dm, a, b: bigA[:, dm * CW + a: dm * CW + b]
        kT_k = lambda k, a, b: bigA[:, k * CW + a: k * CW + b]
        v_t = lambda i, a, b: bigA[:, V_OFF + i * D + a: V_OFF + i * D + b]
        w2h_k = lambda dk, a, b: w2h[:, dk * D + a: dk * D + b]
        rh1T = lambda dm, a, b, cap: bigA[:, dm * cap + a: dm * cap + b]
        rxgT = lambda k, a, b, cap: bigA[:, 16 * cap + k * cap + a:
                                         16 * cap + k * cap + b]
        rge_v = lambda cap, sl=0: bigA[:, 24 * cap + (0 if sl == 0 else 3 * D):
                                       24 * cap + (0 if sl == 0 else 3 * D) + D]
        ryst = lambda s, cap: bigA[:, 24 * cap + D + s * D:
                                   24 * cap + D + (s + 1) * D]
        rb2row = lambda cap: bigA[0:1, 24 * cap + 4 * D:24 * cap + 5 * D]
        W1SL = KT * W1BLK

        def w1v(slot, k, a, b, cap):
            if slot < 2:
                base = slot * W1SL
                return w1blk[:, base + k * W1BLK + a: base + k * W1BLK + b]
            base = 24 * cap + 5 * D + (slot - 2) * W1SL
            return bigA[:, base + k * W1BLK + a: base + k * W1BLK + b]

        def w1v_full(slot, cap):
            if slot < 2:
                return w1blk[:, slot * W1SL:(slot + 1) * W1SL]
            base = 24 * cap + 5 * D + (slot - 2) * W1SL
            return bigA[:, base:base + W1SL]
        bv_row = w2h[0:1, 13056:14080]
        qT_k = lambda k, a, b: w2h[:, k * D + a: k * D + b]
        wqb_k = lambda k, a, b: w2h[:, 8192 + k * 512 + a: 8192 + k * 512 + b]
        scob_s = lambda s, a, b: w2h[:, 12288 + s * 384 + a: 12288 + s * 384 + b]
        w1b_k = lambda slot, k, a, b: w1blk[:, slot * KT * W1BLK + k * W1BLK + a:
                                            slot * KT * W1BLK + k * W1BLK + b]

        def pbank(pi, half):
            return 2 * pi + half

        op("sync", lambda E: E.dma_start(out=id_f[:], in_=idf[:]), "sCs", 16)
        lnrow = yacc[0:1, 0:2 * D]
        onesf = yacc[0:1, 2 * D:2 * D + 512]
        b1_stage = yacc[0:8, 4096:4096 + DFF]
        bq_stage = yacc[0:16, 2 * D + 512:2 * D + 512 + 128]
        op("sync", lambda E: E.dma_start(out=lnrow[:, 0:D], in_=lnw[None, :]), "sCs", 16)
        op("sync", lambda E: E.dma_start(out=lnrow[:, D:2 * D], in_=lnb[None, :]), "sCs", 16)
        op("sync", lambda E: E.dma_start(out=onesf, in_=ones_in[:]), "sCs", 16)
        op("sync", lambda E: E.dma_start(out=b1_stage, in_=b1[:, :]), "sCs", 16)
        op("sync", lambda E: E.dma_start(
            out=bq_stage, in_=bqkv[0:2 * D].rearrange("(t p) -> t p", p=128)), "sCs", 16)
        op("sync", lambda E: E.dma_start(
            out=x_sb.ap().rearrange("p (t n) -> p t n", t=NT_MAX),
            in_=x_in[0:NT_MAX * 128, :].rearrange("(t p) n -> p t n", p=128)), "sX", 16)
        op("gpsimd", lambda E: E.dma_start(
            out=mk_sb.ap().rearrange("p (m n) -> p m n", m=5),
            in_=masks.rearrange("m p n -> p m n")), "sCg", 16)
        op("gpsimd", lambda E: E.dma_start(
            out=wr_sb.ap().rearrange("p (k e) -> p k e", k=KT),
            in_=wr.rearrange("(k p) e -> p k e", p=128)), "sCg", 16)
        op("gpsimd", lambda E: E.dma_start(out=br_row[:], in_=br[None, :]), "sCg", 16)
        op("gpsimd", lambda E: E.dma_start(out=ones_bf[:], in_=ones_in[0:1, 0:128]), "sCg", 16)
        op("gpsimd", lambda E: E.dma_start(out=b2_sb[:], in_=b2[:, :]), "sCg", 16)
        op("gpsimd", lambda E: E.dma_start(out=tri_sb[:], in_=tri[:]), "sCg", 16)
        op("sync", lambda E: E.dma_start(out=bcol[:, 2:4], in_=ecap[:]), "sCs", 16)

        for e in ("tensor", "vector", "scalar"):
            wait(e, "sCs", cnt["sCs"])
            wait(e, "sCg", cnt["sCg"])
        wait("tensor", "sX", cnt["sX"])
        epsv = op("vector", lambda E: E.memset(stats[:, 64:65], LN_EPS), "sV")
        wait("scalar", "sV", epsv)
        op("vector", lambda E: E.tensor_scalar(
            bcol[:, 4:5], bcol[:, 2:3], 511.0, None, op0=ALU.add))
        op("vector", lambda E: E.tensor_scalar(
            bcol[:, 5:6], bcol[:, 3:4], float(CAP - 1), None, op0=ALU.add))
        zv = op("vector", lambda E: E.memset(rge_v(384, 0), 0.0), "sV")
        wait("gpsimd", "sV", zv)
        for zi in range(NE * 512 // 128):
            op("gpsimd", lambda E, zi=zi: E.dma_start(
                out=xg[zi * 128:(zi + 1) * 128, :], in_=rge_v(384, 0)), "sSC", 16)
        wait("scalar", "sSC", cnt["sSC"])
        wait("vector", "sSC", cnt["sSC"])

        for dm in range(32):
            ps = P[dm % 2]
            bank_wait("tensor", [pbank(dm % 2, 0)])
            tv = op("tensor", lambda E, dm=dm, ps=ps: E.transpose(
                ps[:, 0:8], b1_stage[:, dm * 128:(dm + 1) * 128], id_f[0:8, 0:8]), "sT")
            wait("vector", "sT", tv)
            dst = b1_sb.ap().rearrange("p (e m) -> p e m", e=NE)[:, :, dm]
            vv = op("vector", lambda E, d=dst, ps=ps: E.tensor_copy(d, ps[:, 0:8]), "sV")
            bank_release([pbank(dm % 2, 0)], "sV", vv)
        bank_wait("tensor", [4])
        tv = op("tensor", lambda E: E.transpose(P[2][:, 0:16], bq_stage, id_f[0:16, 0:16]), "sT")
        wait("vector", "sT", tv)
        op("vector", lambda E: E.tensor_scalar(bqkvT[:, 0:8], P[2][:, 0:8],
                                               INV_SCALE, None, op0=ALU.mult))
        vv = op("vector", lambda E: E.tensor_copy(bqkvT[:, 8:16], P[2][:, 8:16]), "sV")
        bank_release([4], "sV", vv)
        bank_wait("tensor", [6, 7])
        for half in range(2):
            op("tensor", lambda E, h=half: E.matmul(
                P[3][:, 512 * h:512 * (h + 1)], onesf[:, 0:128],
                lnrow[:, 512 * h:512 * (h + 1)], start=True, stop=True))
        bank_wait("tensor", [4, 5])
        tv = op("tensor", lambda E: E.matmul(
            P[2][:, 0:512], onesf[:, 0:128], lnrow[:, 1024:1536], start=True, stop=True), "sT")
        tv2 = op("tensor", lambda E: E.matmul(
            P[2][:, 512:1024], onesf[:, 0:128], lnrow[:, 1536:2048], start=True, stop=True), "sT")
        wait("vector", "sT", tv2)
        op("vector", lambda E: E.tensor_copy(lnwb[:], P[3][:]))
        vv = op("vector", lambda E: E.tensor_copy(lnbb[:], P[2][:]), "sV")
        bank_release([4, 5, 6, 7], "sV", vv)
        setup_done_v = cnt["sV"]
        setup_cg = cnt["sCg"]

        w1_sems = ["sW1a", "sW1b"]
        w1_slot_free = [[("sV", 0)], [("sV", 0)]]
        w1_wb_tail = [("sWBa", 0), ("sWBb", 0)]
        x_marks = []
        xgT_free = [0]
        w2h_loads = [0]
        b2row_free = [0]
        w1_sems4 = ["sW1a", "sW1b", "sW1c", "sW1d"]
        w1_slot_free4 = [("sV", 0)] * 4
        w2lo_free = [0]
        yst_free = [("sYGa", 0), ("sYGb", 0)]
        w2_wb_tail = []
        wq_wb_tail = []
        w1_loads = [0, 0]
        w2_loads = 0
        q_loads = 0
        x_loads = 1
        stores = 0

        def ln_phase(c, tiles, tag, pre_tile=None):
            tr_slot = [0, 0]
            for ti, t in enumerate(tiles):
                if pre_tile is not None:
                    pre_tile(ti, t)
                    wait("scalar", "sV", cnt["sV"])
                    drain("vector")
                s = (t % 2) * 32
                op("vector", lambda E, t=t, s=s: E.reduce_sum(
                    stats[:, s:s + 1], x_t(t), axis=AX.X))
                av = op("scalar", lambda E, t=t: E.activation(
                    yacc_t(t), x_t(t), AF.Square), "sA")
                wait("vector", "sA", av)
                op("vector", lambda E, t=t, s=s: E.reduce_sum(
                    stats[:, s + 1:s + 2], yacc_t(t), axis=AX.X))
                drain("vector")
                op("vector", lambda E, s=s: E.tensor_scalar(
                    stats[:, s + 2:s + 3], stats[:, s:s + 1], 1.0 / D, None, op0=ALU.mult))
                drain("vector")
                vv = op("vector", lambda E, s=s: E.scalar_tensor_tensor(
                    stats[:, s + 3:s + 4], stats[:, s:s + 1], stats[:, s + 2:s + 3],
                    stats[:, s + 1:s + 2], op0=ALU.mult, op1=ALU.subtract), "sV")
                wait("scalar", "sV", vv)
                av = op("scalar", lambda E, s=s: E.activation(
                    stats[:, s + 4:s + 5], stats[:, s + 3:s + 4], AF.Sqrt,
                    bias=stats[:, 64:65], scale=-1.0 / D), "sA")
                wait("vector", "sA", av)
                op("vector", lambda E, s=s: E.reciprocal(
                    stats[:, s + 5:s + 6], stats[:, s + 4:s + 5]))
                drain("vector")
                if c != 0 and len(tiles) <= 8:
                    xnbuf = xn_bf[:] if ti % 2 == 0 else yacc_t(8)
                    wait("vector", "sT", tr_slot[ti % 2])
                else:
                    xnbuf = xn_bf[:]
                    wait("vector", "sT", tr_slot[0])
                vv = op("vector", lambda E, t=t, s=s, xb=xnbuf: E.tensor_scalar(
                    xb, x_t(t), stats[:, s + 2:s + 3], stats[:, s + 5:s + 6],
                    op0=ALU.subtract, op1=ALU.mult), "sV")
                wait("tensor", "sV", vv)
                last_t = None
                for k in range(KT):
                    pi = (t * KT + k) % 2
                    bank_wait("tensor", [pbank(pi, 0)])
                    tvv = op("tensor", lambda E, k=k, pi=pi, xb=xnbuf: E.transpose(
                        P[pi][:, 0:128], xb[:, k * 128:(k + 1) * 128], id_f[:]), "sT")
                    wait("vector", "sT", tvv)
                    cv = op("vector", lambda E, k=k, t=t, pi=pi: E.tensor_copy(
                        xnT_k(k, t * 128, (t + 1) * 128), P[pi][:, 0:128]), "sV")
                    bank_release([pbank(pi, 0)], "sV", cv)
                    last_t = tvv
                if c != 0 and len(tiles) <= 8:
                    tr_slot[ti % 2] = last_t
                else:
                    tr_slot[0] = last_t

        def lnr_phase(c, tiles, tag, cap, xw=None):
            ecap_c = 2 if cap == 512 else 3
            bound_c = 4 if cap == 512 else 5
            routed = True
            tr_slot = [0, 0]
            sc_slot = [("sSC", 0), ("sSC", 0)]
            bz = op("vector", lambda E: E.memset(bcol[:, 0:1], 0.0), "sV")
            drain("vector")
            for ti, t in enumerate(tiles):
                s = (t % 2) * 32
                if xw is not None and t // 2 < len(xw):
                    wait("vector", "sX", xw[t // 2])
                    wait("scalar", "sX", xw[t // 2])
                op("vector", lambda E, t=t, s=s: E.reduce_sum(
                    stats[:, s:s + 1], x_t(t), axis=AX.X))
                av = op("scalar", lambda E, t=t: E.activation(
                    yacc_t(t), x_t(t), AF.Square), "sA")
                wait("vector", "sA", av)
                op("vector", lambda E, t=t, s=s: E.reduce_sum(
                    stats[:, s + 1:s + 2], yacc_t(t), axis=AX.X))
                drain("vector")
                op("vector", lambda E, s=s: E.tensor_scalar(
                    stats[:, s + 2:s + 3], stats[:, s:s + 1], 1.0 / D, None, op0=ALU.mult))
                drain("vector")
                vv = op("vector", lambda E, s=s: E.scalar_tensor_tensor(
                    stats[:, s + 3:s + 4], stats[:, s:s + 1], stats[:, s + 2:s + 3],
                    stats[:, s + 1:s + 2], op0=ALU.mult, op1=ALU.subtract), "sV")
                wait("scalar", "sV", vv)
                av = op("scalar", lambda E, s=s: E.activation(
                    stats[:, s + 4:s + 5], stats[:, s + 3:s + 4], AF.Sqrt,
                    bias=stats[:, 64:65], scale=-1.0 / D), "sA")
                wait("vector", "sA", av)
                op("vector", lambda E, s=s: E.reciprocal(
                    stats[:, s + 5:s + 6], stats[:, s + 4:s + 5]))
                drain("vector")
                pingpong = len(tiles) <= 8
                slotB = 8 if tiles[0] == 0 else 0
                if pingpong:
                    xslot = ti % 2
                    xnbuf = xn_bf[:] if xslot == 0 else yacc_t(slotB)
                    wait("vector", "sT", tr_slot[xslot])
                else:
                    xslot = 0
                    xnbuf = xn_bf[:]
                    wait("vector", "sT", tr_slot[0])
                wait("vector", sc_slot[xslot][0], sc_slot[xslot][1])
                vv = op("vector", lambda E, t=t, s=s, xb=xnbuf: E.tensor_scalar(
                    xb, x_t(t), stats[:, s + 2:s + 3], stats[:, s + 5:s + 6],
                    op0=ALU.subtract, op1=ALU.mult), "sV")
                wait("tensor", "sV", vv)
                last_t = None
                for k in range(KT):
                    pi = (t * KT + k) % 2
                    bank_wait("tensor", [pbank(pi, 0)])
                    tvv = op("tensor", lambda E, k=k, pi=pi, xb=xnbuf: E.transpose(
                        P[pi][:, 0:128], xb[:, k * 128:(k + 1) * 128], id_f[:]), "sT")
                    wait("vector", "sT", tvv)
                    cv = op("vector", lambda E, k=k, t=t, pi=pi: E.tensor_copy(
                        xnT_k(k, t * 128, (t + 1) * 128), P[pi][:, 0:128]), "sV")
                    bank_release([pbank(pi, 0)], "sV", cv)
                    last_t = tvv
                if pingpong:
                    tr_slot[ti % 2] = last_t
                else:
                    tr_slot[0] = last_t
                s = (t % 2) * 32
                half = ti % 2
                lg_col = 512 * half
                bank_wait("tensor", [4 + half])
                wait("tensor", "sV", cv)
                for k in range(KT):
                    op("tensor", lambda E, k=k, t=t, lc=lg_col: E.matmul(
                        P[2][:, lc:lc + 8], xnT_k(k, t * 128, (t + 1) * 128),
                        wr_sb[:, k * 8:(k + 1) * 8], start=(k == 0), stop=False))
                tv = op("tensor", lambda E, lc=lg_col: E.matmul(
                    P[2][:, lc:lc + 8], ones_bf[0:1, 0:128], br_row[:],
                    start=False, stop=True), "sT")
                wait("vector", "sT", tv)
                vv = op("vector", lambda E, s=s, lc=lg_col: E.tensor_copy(
                    stats[:, s + 8:s + 16], P[2][:, lc:lc + 8]), "sV")
                wait("scalar", "sV", vv)
                av = op("scalar", lambda E, s=s: E.activation(
                    stats[:, s + 16:s + 24], stats[:, s + 8:s + 16], AF.Exp), "sA")
                drain("vector")
                op("vector", lambda E, s=s: E.reduce_max(
                    stats[:, s:s + 1], stats[:, s + 8:s + 16], axis=AX.X))
                wait("vector", "sA", av)
                op("vector", lambda E, s=s: E.reduce_sum(
                    stats[:, s + 2:s + 3], stats[:, s + 16:s + 24], axis=AX.X))
                drain("vector")
                op("vector", lambda E, s=s: E.reciprocal(
                    stats[:, s + 3:s + 4], stats[:, s + 2:s + 3]))
                op("vector", lambda E, s=s: E.tensor_scalar(
                    stats[:, s + 24:s + 32], stats[:, s + 8:s + 16], stats[:, s:s + 1],
                    None, op0=ALU.is_equal))
                drain("vector")
                op("vector", lambda E, s=s: E.tensor_tensor(
                    stats[:, s + 16:s + 24], stats[:, s + 16:s + 24],
                    stats[:, s + 24:s + 32], op=ALU.mult))
                drain("vector")
                lv = op("vector", lambda E, t=t, s=s: E.tensor_scalar(
                    S_t(t), stats[:, s + 16:s + 24], stats[:, s + 3:s + 4],
                    0.5, op0=ALU.mult, op1=ALU.mult), "sV")
                wait("tensor", "sV", lv)
                if routed:
                    ms = ti % 2
                    mv2 = op("vector", lambda E, s=s, ms=ms: E.tensor_copy(
                        Mbf[:, ms * 8:(ms + 1) * 8], stats[:, s + 24:s + 32]), "sV")
                    wait("tensor", "sV", mv2)
                    wait("tensor", "sCg", setup_cg)
                    tv2 = op("tensor", lambda E, ms=ms, lc=lg_col: E.matmul(
                        P[2][0:8, lc + 160:lc + 296], Mbf[:, ms * 8:(ms + 1) * 8],
                        tri_sb[:], start=True, stop=True), "sT")
                    wait("vector", "sT", tv2)
                    op("vector", lambda E: E.scalar_tensor_tensor(
                        bcol[:, 1:2], bcol[:, 0:1], -1.0, bcol[:, ecap_c:ecap_c + 1],
                        op0=ALU.add, op1=ALU.add))
                    drain("vector")
                    op("vector", lambda E, ms=ms, lc=lg_col: E.tensor_scalar(
                        prefs[:, ms * 128:(ms + 1) * 128],
                        P[2][0:8, lc + 160:lc + 288], bcol[:, 1:2], None, op0=ALU.add))
                    drain("vector")
                    vv3 = op("vector", lambda E, ms=ms: E.tensor_scalar(
                        prefs[:, ms * 128:(ms + 1) * 128],
                        prefs[:, ms * 128:(ms + 1) * 128], bcol[:, bound_c:bound_c + 1], None, op0=ALU.min), "sV")
                    op("vector", lambda E, lc=lg_col: E.tensor_scalar(
                        bcol[:, 0:1], P[2][0:8, lc + 288:lc + 289], bcol[:, 0:1],
                        None, op0=ALU.add))
                    drain("vector")
                    wait("tensor", "sV", vv3)
                    tv3 = op("tensor", lambda E, ms=ms, lc=lg_col: E.transpose(
                        P[2][:, lc + 304:lc + 312], prefs[:, ms * 128:(ms + 1) * 128],
                        id_f[0:8, 0:8]), "sT")
                    wait("vector", "sT", tv3)
                    op("vector", lambda E, s=s, ms=ms, lc=lg_col: E.tensor_tensor(
                        ptk[:, ms * 8:(ms + 1) * 8], P[2][:, lc + 304:lc + 312],
                        stats[:, s + 24:s + 32], op=ALU.mult))
                    drain("vector")
                    op("vector", lambda E, s=s, ms=ms: E.reduce_sum(
                        stats[:, s + 6:s + 7], ptk[:, ms * 8:(ms + 1) * 8], axis=AX.X))
                    drain("vector")
                    op("vector", lambda E, s=s, ti=ti: E.tensor_copy(
                        dint[:, ti:ti + 1], stats[:, s + 6:s + 7]))
                    dv2 = op("vector", lambda E, t=t, ti=ti: E.reduce_sum(
                        Ssel_sb[:, ti:ti + 1], S_t(t), axis=AX.X), "sV")
                    bank_release([4 + half], "sV", dv2)
                    wait("gpsimd", "sV", dv2)
                    ssem = "sSC" if ti % 2 == 0 else "sSC2"
                    scv = op("gpsimd", lambda E, ti=ti, xb=xnbuf: E.indirect_dma_start(
                        out=xg, out_offset=bass.IndirectOffsetOnAxis(ap=dint[:, ti:ti + 1], axis=0),
                        in_=xb, in_offset=None), ssem, 16)
                    sc_slot[xslot] = (ssem, scv)


        def prefetch_e0(c):
            pre = {}
            blks = []
            for blk in range(2):
                slot = blk
                ssem = w1_sems[slot]
                for fs, fv in w1_slot_free[slot]:
                    wait("gpsimd", fs, fv)
                ldv = inc(ssem, 16)
                blks.append((ssem, ldv))
                ops["gpsimd"].append(
                    lambda E, bg=blk, slot=slot, ss=ssem: E.dma_start(
                        out=w1blk[:, slot * KT * W1BLK:(slot + 1) * KT * W1BLK],
                        in_=w1s[0, bg]
                    ).then_inc(SEM[ss], 16))
            pre["blks"] = blks
            wait("scalar", "sT", cnt["sT"])
            for fs, fv in w2_wb_tail:
                wait("scalar", fs, fv)
            for fs, fv in wq_wb_tail:
                wait("scalar", fs, fv)
            w2q = {}
            for j in range(2):
                wsem = "sW2h" if j == 0 else "sW2g"
                w2q["lo" if j == 0 else "hi"] = (wsem, inc(wsem, 16))
                ops["scalar"].append(
                    lambda E, j=j, wsem=wsem: E.dma_start(
                        out=w2h[:, j * 8192:(j + 1) * 8192],
                        in_=w2s[0, 0][:, j * 8192:(j + 1) * 8192]
                    ).then_inc(SEM[wsem], 16))
            pre["w2q"] = w2q
            return pre

        def routed_ffn_phase(c, tiles, tag, cap=CAP, wsrc="bf16", pre=None,
                             defer_gather=False):
            nonlocal w2_loads, q_loads
            nst = cap // 128
            prs = [2, 3, 0, 1][:nst]
            wb_sems = {0: "sWBa", 1: "sWBb"}
            scat_done = cnt["sSC"]
            scat_done2 = cnt["sSC2"]
            if wsrc == "bf16":
                w1_slot_free4[2] = ("sT", cnt["sT"])
                w1_slot_free4[3] = ("sT", cnt["sT"])
            yga, ygb = 0, 0
            for e in range(n_experts):
                wait("scalar", "sSC", scat_done)
                wait("scalar", "sSC2", scat_done2)
                if xgT_free[0]:
                    wait("scalar", "sT", xgT_free[0])
                for dq in range(KT):
                    inc("sXG", 16)
                    ops["scalar"].append(lambda E, e=e, dq=dq, cap=cap: E.dma_start(
                        out=rxgT(dq, 0, cap, cap), in_=xg[e * cap:(e + 1) * cap, dq * 128:(dq + 1) * 128],
                        transpose=True).then_inc(SEM["sXG"], 16))
                xgq = cnt["sXG"]
                wait("tensor", "sXG", xgq)
                wait("gpsimd", "sT", b2row_free[0])
                b2v = op("gpsimd", lambda E, e=e, cap=cap: E.dma_start(
                    out=rb2row(cap), in_=b2[e:e + 1, :]), "sQ", 16)
                wait("tensor", "sQ", b2v)
                for h in range(2):
                    per_blk = BDM
                    nslot = 4 if wsrc == "bf16" else 2
                    sT0 = cnt["sT"]
                    blk_ld = []
                    for blk in range(NBLK):
                        slot = blk % nslot
                        ssem = w1_sems4[slot] if nslot == 4 else w1_sems[slot]
                        if e == 0 and h == 0 and blk < 2 and pre is not None:
                            blk_ld.append(pre["blks"][blk])
                            continue
                        if nslot == 4:
                            fs, fv = w1_slot_free4[slot]
                            wait("gpsimd", fs, fv)
                        elif blk >= 2:
                            wait("gpsimd", "sT", sT0 + (blk - 1) * per_blk)
                        else:
                            for fs, fv in w1_slot_free[slot]:
                                wait("gpsimd", fs, fv)
                        if wsrc == "f32wb" and blk >= 2:
                            fs, fv = w1_wb_tail[slot]
                            wait("gpsimd", fs, fv)
                        ldv = inc(ssem, 16)
                        blk_ld.append((ssem, ldv))
                        bg = h * NBLK + blk
                        if wsrc == "f32wb":
                            cblk = h * 2048 + blk * W1BLK
                            ops["gpsimd"].append(
                                lambda E, cb=cblk, slot=slot, ss=ssem, e=e: E.dma_start(
                                    out=w1blk.ap().rearrange("p (s k n) -> p s k n", s=2, k=KT)[:, slot],
                                    in_=w1[e].rearrange("(k p) n -> p k n", p=128)[:, :, cb:cb + W1BLK]
                                ).then_inc(SEM[ss], 16))
                            wbs = wb_sems[slot]
                            wait("sync", ssem, ldv)
                            wbv = inc(wbs, 16)
                            ops["sync"].append(
                                lambda E, bg=bg, slot=slot, wb=wbs, e=e: E.dma_start(
                                    out=w1s[e, bg],
                                    in_=w1blk[:, slot * KT * W1BLK:(slot + 1) * KT * W1BLK]
                                ).then_inc(SEM[wb], 16))
                            w1_wb_tail[slot] = (wbs, wbv)
                        else:
                            ops["gpsimd"].append(
                                lambda E, bg=bg, slot=slot, ss=ssem, e=e, cap=cap: E.dma_start(
                                    out=w1v_full(slot, cap),
                                    in_=w1s[e, bg]
                                ).then_inc(SEM[ss], 16))
                    if nslot == 4:
                        for blk in range(NBLK):
                            w1_slot_free4[blk % 4] = ("sT", sT0 + (blk + 1) * per_blk)
                        w1_slot_free[0] = [w1_slot_free4[0]]
                        w1_slot_free[1] = [w1_slot_free4[1]]
                    else:
                        w1_slot_free[0] = [("sT", sT0 + (NBLK - 1) * per_blk)] + \
                            ([w1_wb_tail[0]] if wsrc == "f32wb" else [])
                        w1_slot_free[1] = [("sT", sT0 + NBLK * per_blk)] + \
                            ([w1_wb_tail[1]] if wsrc == "f32wb" else [])
                        w1_slot_free4[0] = w1_slot_free[0][0]
                        w1_slot_free4[1] = w1_slot_free[1][0]
                    if e == 0 and h == 0 and pre is not None:
                        w2q = pre["w2q"]
                        skip_w2 = True
                    else:
                        skip_w2 = False
                    if not skip_w2 and wsrc == "f32wb":
                        wait("gpsimd", "sT", cnt["sT"])
                        for fs, fv in w2_wb_tail:
                            wait("gpsimd", fs, fv)
                        w2_wb_tail.clear()
                        w2src = w2[e].rearrange("(dk p) n -> p dk n", p=128)
                        w2q = {}
                        for j in range(4):
                            wsem = "sW2i" if j < 2 else "sW2j"
                            w2q["lo" if j < 2 else "hi"] = (wsem, inc(wsem, 16))
                            ops["gpsimd"].append(
                                lambda E, h=h, j=j, ws=w2src, wsem=wsem: E.dma_start(
                                    out=w2h.ap().rearrange("p (dk n) -> p dk n", dk=NDM)[:, 4 * j:4 * j + 4, :],
                                    in_=ws[:, h * NDM + 4 * j: h * NDM + 4 * j + 4, :]
                                ).then_inc(SEM[wsem], 16))
                        wait("sync", w2q["lo"][0], w2q["lo"][1])
                        wait("sync", w2q["hi"][0], w2q["hi"][1])
                        for j in range(2):
                            wbv = inc("sWB2", 16)
                            ops["sync"].append(
                                lambda E, h=h, j=j, e=e: E.dma_start(
                                    out=w2s[e, h][:, j * 8192:(j + 1) * 8192],
                                    in_=w2h[:, j * 8192:(j + 1) * 8192]
                                ).then_inc(SEM["sWB2"], 16))
                        w2_wb_tail.append(("sWB2", wbv))
                    elif not skip_w2:
                        for fs, fv in w2_wb_tail:
                            wait("scalar", fs, fv)
                        for fs, fv in wq_wb_tail:
                            wait("scalar", fs, fv)
                        w2q = {}
                        for j in range(2):
                            if j == 0 and w2lo_free[0]:
                                wait("scalar", "sT", w2lo_free[0])
                            if j == 1:
                                wait("scalar", "sT", cnt["sT"])
                            wsem = "sW2h" if j == 0 else "sW2g"
                            w2q["lo" if j == 0 else "hi"] = (wsem, inc(wsem, 16))
                            ops["scalar"].append(
                                lambda E, h=h, j=j, e=e, wsem=wsem: E.dma_start(
                                    out=w2h[:, j * 8192:(j + 1) * 8192],
                                    in_=w2s[e, h][:, j * 8192:(j + 1) * 8192]
                                ).then_inc(SEM[wsem], 16))
                    for blk in range(NBLK):
                        ssem, ldv = blk_ld[blk]
                        wait("tensor", ssem, ldv)
                        for dml in range(BDM):
                            dm = blk * BDM + dml
                            ps = dm % 4
                            pt, pcol = ps // 2, (ps % 2) * 512
                            bank_wait("tensor", [pbank(pt, ps % 2)])
                            for k in range(KT):
                                fn = lambda E, pt=pt, pcol=pcol, slot=blk % nslot, k=k, dml=dml, cap=cap: E.matmul(
                                    P[pt][:, pcol:pcol + cap],
                                    w1v(slot, k, dml * 128, dml * 128 + 128, cap),
                                    rxgT(k, 0, cap, cap), start=(k == 0), stop=(k == KT - 1))
                                if k == KT - 1:
                                    tv = op("tensor", fn, "sT")
                                else:
                                    op("tensor", fn)
                            wait("scalar", "sT", tv)
                            bcol_i = e * 32 + h * 16 + dm
                            av = op("scalar", lambda E, dm=dm, pt=pt, pcol=pcol, bc=bcol_i, cap=cap: E.activation(
                                rh1T(dm, 0, cap, cap), P[pt][:, pcol:pcol + cap],
                                AF.Relu, bias=b1_sb[:, bc:bc + 1], scale=1.0), "sA")
                            bank_release([pbank(pt, ps % 2)], "sA", av)
                    if h == 1:
                        xgT_free[0] = cnt["sT"]
                    h1_done = cnt["sA"]
                    wait("tensor", "sA", h1_done)
                    evs = []
                    for st in range(nst):
                        pi = prs[st]
                        bank_wait("tensor", [pbank(pi, 0), pbank(pi, 1)])
                        if st == 0:
                            wait("tensor", w2q["lo"][0], w2q["lo"][1])
                        for dk in range(NDM):
                            if st == 0 and dk == 8:
                                wait("tensor", w2q["hi"][0], w2q["hi"][1])
                            last = (dk == NDM - 1 and h == 0)
                            op("tensor", lambda E, pi=pi, dk=dk, st=st, cap=cap, sp=last: E.matmul(
                                P[pi][:, 0:512], rh1T(dk, st * 128, st * 128 + 128, cap),
                                w2h_k(dk, 0, 512), start=(dk == 0), stop=sp))
                            fn = lambda E, pi=pi, dk=dk, st=st, cap=cap, sp=last: E.matmul(
                                P[pi][:, 512:1024], rh1T(dk, st * 128, st * 128 + 128, cap),
                                w2h_k(dk, 512, 1024), start=(dk == 0), stop=sp)
                            if dk == NDM - 1 and h == 0:
                                tv = op("tensor", fn, "sT")
                            elif st == nst - 1 and dk == 7:
                                w2lo_free[0] = op("tensor", fn, "sT")
                            else:
                                op("tensor", fn)
                        if h == 1:
                            op("tensor", lambda E, pi=pi, cap=cap: E.matmul(
                                P[pi][:, 0:512], ones_bf[0:1, 0:128],
                                rb2row(cap)[0:1, 0:512], start=False, stop=True))
                            tv = op("tensor", lambda E, pi=pi, cap=cap: E.matmul(
                                P[pi][:, 512:1024], ones_bf[0:1, 0:128],
                                rb2row(cap)[0:1, 512:1024], start=False, stop=True), "sT")
                            b2row_free[0] = tv
                        wait("vector", "sT", tv)
                        drain("vector")
                        if h == 0:
                            vv = op("vector", lambda E, st=st, pi=pi: E.tensor_copy(
                                yacc_t(st), P[pi][:]), "sV")
                        else:
                            vv = op("vector", lambda E, st=st, pi=pi: E.tensor_tensor(
                                yacc_t(st), yacc_t(st), P[pi][:], op=ALU.add), "sV")
                            ysem = "sYGa" if st % 2 == 0 else "sYGb"
                            fs, fv = yst_free[st % 2]
                            wait("vector", fs, fv)
                            drain("vector")
                            cv = op("vector", lambda E, st=st, cap=cap: E.tensor_copy(
                                ryst(st % 2, cap), yacc_t(st)), "sV")
                            wait("sync", "sV", cv)
                            gv = inc(ysem, 16)
                            ops["sync"].append(lambda E, e=e, st=st, ys=ysem, cap=cap: E.dma_start(
                                out=yg[e * cap + st * 128: e * cap + (st + 1) * 128, :],
                                in_=ryst(st % 2, cap)).then_inc(SEM[ys], 16))
                            yst_free[st % 2] = (ysem, gv)
                        bank_release([pbank(pi, 0), pbank(pi, 1)], "sV", vv)
            ya, yb = cnt["sYGa"], cnt["sYGb"]
            stt_done = [0, 0]

            def emit_gather(i, t):
                wait("gpsimd", "sYGa", ya)
                wait("gpsimd", "sYGb", yb)
                sl = i % 2
                gsem = "sGB" if sl == 0 else "sGB2"
                wait("gpsimd", "sV", stt_done[sl])
                gv = op("gpsimd", lambda E, i=i, cap=cap, sl=sl: E.indirect_dma_start(
                    out=rge_v(cap, sl), out_offset=None, in_=yg,
                    in_offset=bass.IndirectOffsetOnAxis(ap=dint[:, i:i + 1], axis=0)), gsem, 16)
                wait("vector", gsem, gv)
                sv = op("vector", lambda E, t=t, i=i, cap=cap, sl=sl: E.scalar_tensor_tensor(
                    x_t(t), rge_v(cap, sl), Ssel_sb[:, i:i + 1], x_t(t),
                    op0=ALU.mult, op1=ALU.add), "sV")
                stt_done[sl] = sv

            if defer_gather:
                return emit_gather
            for i, t in enumerate(tiles):
                emit_gather(i, t)

        def attn_phase(c, tag, gb=None):
            nonlocal q_loads
            nt, own0 = CHUNKS[c]
            ln_tiles = list(range(nt)) if c == 0 else list(range(8))
            ln_phase(c, ln_tiles, tag + "_ln2", pre_tile=gb)
            barrier()
            if c > 0:
                wait("sync", "sT", cnt["sT"])
                wait("sync", "sA", cnt["sA"])
                wait("sync", "sV", cnt["sV"])
                kv_in = inc("sKV", 32)
                ops["sync"].append(lambda E: E.dma_start(
                    out=bigA.ap().rearrange("p (k c) -> p k c", c=CW)[:, 0:8, 0:256],
                    in_=kvt[:, 0:2048].rearrange("p (k c) -> p k c", c=256)).then_inc(SEM["sKV"], 16))
                ops["sync"].append(lambda E: E.dma_start(
                    out=bigA[:, V_OFF:V_OFF + 2 * D],
                    in_=kvt[:, 2048:2048 + 2 * D]).then_inc(SEM["sKV"], 16))
                wait("tensor", "sKV", kv_in)
            q_src0 = own0 * 128
            if c == 0:
                k_srcs = [(0, 512), (512, 512), (1024, 256)]
                k_dst0 = 0
            else:
                k_srcs = [(0, 512), (512, 512)]
                k_dst0 = 256
            for b in range(6):
                wait("gpsimd", "sT", cnt["sT"])
                if b == 4:
                    q_loads = inc("sQ", 16)
                    ops["gpsimd"].append(lambda E: E.dma_start(
                        out=bv_row, in_=bqkv[None, 2 * D:3 * D]).then_inc(SEM["sQ"], 16))
                for fs, fv in wq_wb_tail:
                    wait("gpsimd", fs, fv)
                wq_wb_tail.clear()
                for fs, fv in w2_wb_tail:
                    wait("gpsimd", fs, fv)
                q_loads = inc("sQ", 16)
                if c == 0:
                    ops["gpsimd"].append(lambda E, b=b: E.dma_start(
                        out=w2h.ap()[:, 8192:12288].rearrange("p (k n) -> p k n", k=KT),
                        in_=wqkv.rearrange("(k p) n -> p k n", p=128)[:, :, b * 512:(b + 1) * 512]
                    ).then_inc(SEM["sQ"], 16))
                    wait("sync", "sQ", q_loads)
                    wbv = inc("sWBq", 16)
                    ops["sync"].append(lambda E, b=b: E.dma_start(
                        out=wqs[b], in_=w2h[:, 8192:12288]).then_inc(SEM["sWBq"], 16))
                    wq_wb_tail.append(("sWBq", wbv))
                else:
                    ops["gpsimd"].append(lambda E, b=b: E.dma_start(
                        out=w2h[:, 8192:12288], in_=wqs[b]).then_inc(SEM["sQ"], 16))
                wait("tensor", "sQ", q_loads)
                if b < 2:
                    for dql in range(4):
                        dq = 4 * b + dql
                        pi = dq % 2
                        bank_wait("tensor", [pbank(pi, 0), pbank(pi, 1)])
                        for k in range(KT):
                            for g in range(2):
                                fn = lambda E, pi=pi, g=g, k=k, dql=dql, qs=q_src0: E.matmul(
                                    P[pi][:, g * 512:(g + 1) * 512],
                                    wqb_k(k, dql * 128, dql * 128 + 128),
                                    xnT_k(k, qs + g * 512, qs + (g + 1) * 512),
                                    start=(k == 0), stop=(k == KT - 1))
                                if k == KT - 1 and g == 1:
                                    tv = op("tensor", fn, "sT")
                                else:
                                    op("tensor", fn)
                        wait("scalar", "sT", tv)
                        av = op("scalar", lambda E, dq=dq, pi=pi: E.activation(
                            qT_k(dq, 0, 1024), P[pi][:], AF.Identity,
                            bias=bqkvT[:, dq:dq + 1], scale=INV_SCALE), "sA")
                        bank_release([pbank(pi, 0), pbank(pi, 1)], "sA", av)
                elif b < 4:
                    for dkl in range(4):
                        dk = 4 * (b - 2) + dkl
                        for gi, (sc, sw) in enumerate(k_srcs):
                            ps = (dk * len(k_srcs) + gi) % 4
                            pt, pcol = ps // 2, (ps % 2) * 512
                            bank_wait("tensor", [pbank(pt, ps % 2)])
                            for k in range(KT):
                                fn = lambda E, pt=pt, pcol=pcol, sw=sw, k=k, dkl=dkl, sc=sc: E.matmul(
                                    P[pt][:, pcol:pcol + sw],
                                    wqb_k(k, dkl * 128, dkl * 128 + 128),
                                    xnT_k(k, sc, sc + sw),
                                    start=(k == 0), stop=(k == KT - 1))
                                if k == KT - 1:
                                    tv = op("tensor", fn, "sT")
                                else:
                                    op("tensor", fn)
                            wait("scalar", "sT", tv)
                            av = op("scalar", lambda E, dk=dk, pt=pt, pcol=pcol, sw=sw,
                                    d0=k_dst0 + sc: E.activation(
                                kT_k(dk, d0, d0 + sw), P[pt][:, pcol:pcol + sw], AF.Identity,
                                bias=bqkvT[:, 8 + dk:9 + dk], scale=1.0), "sA")
                            bank_release([pbank(pt, ps % 2)], "sA", av)
                else:
                    vtiles = list(range(nt)) if c == 0 else list(range(2, 10))
                    for i, vt in enumerate(vtiles):
                        ps = i % 4
                        pt, pcol = ps // 2, (ps % 2) * 512
                        bank_wait("tensor", [pbank(pt, ps % 2)])
                        src_c = vt * 128 if c == 0 else (vt - 2) * 128
                        for k in range(KT):
                            op("tensor", lambda E, pt=pt, pcol=pcol, k=k, sc=src_c: E.matmul(
                                P[pt][:, pcol:pcol + 512],
                                xnT_k(k, sc, sc + 128),
                                wqb_k(k, 0, 512), start=(k == 0), stop=False))
                        tv = op("tensor", lambda E, pt=pt, pcol=pcol, b=b: E.matmul(
                            P[pt][:, pcol:pcol + 512], ones_bf[0:1, 0:128],
                            bv_row[0:1, (b - 4) * 512:(b - 3) * 512],
                            start=False, stop=True), "sT")
                        wait("vector", "sT", tv)
                        vv = op("vector", lambda E, vt=vt, b=b, pt=pt, pcol=pcol: E.tensor_copy(
                            v_t(vt, (b - 4) * 512, (b - 3) * 512), P[pt][:, pcol:pcol + 512]), "sV")
                        bank_release([pbank(pt, ps % 2)], "sV", vv)
            if c < n_chunks - 1:
                wait("sync", "sA", cnt["sA"])
                wait("sync", "sV", cnt["sV"])
                kv_out = inc("sKV", 32)
                ops["sync"].append(lambda E: E.dma_start(
                    out=kvt[:, 0:2048].rearrange("p (k c) -> p k c", c=256),
                    in_=bigA.ap().rearrange("p (k c) -> p k c", c=CW)[:, 0:8, 1024:1280]).then_inc(SEM["sKV"], 16))
                ops["sync"].append(lambda E: E.dma_start(
                    out=kvt[:, 2048:2048 + 2 * D],
                    in_=bigA[:, V_OFF + 8 * D:V_OFF + 10 * D]).then_inc(SEM["sKV"], 16))
                wait("scalar", "sKV", kv_out)
                wait("tensor", "sKV", kv_out)
                wait("vector", "sKV", kv_out)
            spare = [0, 1] if c == 0 else [8, 9]
            def emit_scores(qt):
                sslot = qt % 2
                bank_wait("tensor", [4 + sslot])
                for kj in range(3):
                    for dq in range(KT):
                        fn = lambda E, sslot=sslot, kj=kj, dq=dq, qt=qt: E.matmul(
                            P[2][:, sslot * 512 + kj * 128: sslot * 512 + kj * 128 + 128],
                            kT_k(dq, (qt + kj) * 128, (qt + kj + 1) * 128),
                            qT_k(dq, qt * 128, (qt + 1) * 128),
                            start=(dq == 0), stop=(dq == KT - 1))
                        if dq == KT - 1:
                            tv = op("tensor", fn, "sT")
                        else:
                            op("tensor", fn)
                wait("vector", "sT", tv)
                if qt - 2 in av_done:
                    wait("vector", "sT", av_done[qt - 2])
                MKCOL = {MASK_MU: 0, MASK_ML: 1, MASK_A: 2, MASK_B: 3, MASK_C: 4}
                for kj in range(3):
                    if qt == 0 and c == 0:
                        midx = [MASK_A, MASK_B, MASK_MU][kj]
                    elif qt == 1 and c == 0:
                        midx = [MASK_C, MASK_ONES, MASK_MU][kj]
                    else:
                        midx = [MASK_ML, MASK_ONES, MASK_MU][kj]
                    if midx == MASK_ONES:
                        mv = op("vector", lambda E, sslot=sslot, kj=kj: E.tensor_copy(
                            scob_s(sslot, kj * 128, (kj + 1) * 128),
                            P[2][:, sslot * 512 + kj * 128: sslot * 512 + kj * 128 + 128]), "sV")
                    else:
                        mc = MKCOL[midx]
                        mv = op("vector", lambda E, sslot=sslot, kj=kj, mc=mc: E.tensor_tensor(
                            scob_s(sslot, kj * 128, (kj + 1) * 128),
                            P[2][:, sslot * 512 + kj * 128: sslot * 512 + kj * 128 + 128],
                            mk_sb[:, mc * 128:(mc + 1) * 128], op=ALU.mult), "sV")
                bank_release([4 + sslot], "sV", mv)
                mask_done[qt] = mv

            def emit_av_ln(qt):
                xq = own0 + qt
                sslot = qt % 2
                pi = 3 if qt % 2 == 0 else 0
                bank_wait("tensor", [pbank(pi, 0), pbank(pi, 1)])
                wait("tensor", "sV", mask_done[qt])
                for kj in range(3):
                    for g in range(2):
                        fn = lambda E, pi=pi, g=g, sslot=sslot, kj=kj, qt=qt: E.matmul(
                            P[pi][:, g * 512:(g + 1) * 512],
                            scob_s(sslot, kj * 128, (kj + 1) * 128),
                            v_t(qt + kj, g * 512, (g + 1) * 512),
                            start=(kj == 0), stop=(kj == 2))
                        if kj == 2 and g == 1:
                            tv = op("tensor", fn, "sT")
                        else:
                            op("tensor", fn)
                av_done[qt] = tv
                wait("vector", "sT", tv)
                att = yacc_t(spare[0])
                sq = yacc_t(spare[1])
                s = (qt % 2) * 32
                avv = op("vector", lambda E, att=att, pi=pi: E.tensor_copy(att, P[pi][:]), "sV")
                bank_release([pbank(pi, 0), pbank(pi, 1)], "sV", avv)
                drain("vector")
                op("vector", lambda E, att=att, s=s: E.reduce_sum(
                    stats[:, s:s + 1], att, axis=AX.X))
                wait("scalar", "sV", avv)
                av = op("scalar", lambda E, att=att, sq=sq: E.activation(
                    sq, att, AF.Square), "sA")
                wait("vector", "sA", av)
                op("vector", lambda E, sq=sq, s=s: E.reduce_sum(
                    stats[:, s + 1:s + 2], sq, axis=AX.X))
                drain("vector")
                op("vector", lambda E, s=s: E.tensor_scalar(
                    stats[:, s + 2:s + 3], stats[:, s:s + 1], 1.0 / D, None, op0=ALU.mult))
                drain("vector")
                vv = op("vector", lambda E, s=s: E.scalar_tensor_tensor(
                    stats[:, s + 3:s + 4], stats[:, s:s + 1], stats[:, s + 2:s + 3],
                    stats[:, s + 1:s + 2], op0=ALU.mult, op1=ALU.subtract), "sV")
                wait("scalar", "sV", vv)
                av = op("scalar", lambda E, s=s: E.activation(
                    stats[:, s + 4:s + 5], stats[:, s + 3:s + 4], AF.Sqrt,
                    bias=stats[:, 64:65], scale=-1.0 / D), "sA")
                wait("vector", "sA", av)
                op("vector", lambda E, s=s: E.reciprocal(
                    stats[:, s + 5:s + 6], stats[:, s + 4:s + 5]))
                drain("vector")
                op("vector", lambda E, att=att, sq=sq, s=s: E.tensor_scalar(
                    sq, att, stats[:, s + 2:s + 3], stats[:, s + 5:s + 6],
                    op0=ALU.subtract, op1=ALU.mult))
                drain("vector")
                op("vector", lambda E, sq=sq: E.tensor_tensor(sq, sq, lnwb[:], op=ALU.mult))
                drain("vector")
                op("vector", lambda E, sq=sq: E.tensor_tensor(sq, sq, lnbb[:], op=ALU.add))
                drain("vector")
                op("vector", lambda E, sq=sq, xq=xq: E.tensor_tensor(
                    x_t(xq), x_t(xq), sq, op=ALU.add), "sV")

            mask_done = {}
            av_done = {}
            for qt in range(8):
                emit_scores(qt)
                if qt >= 1:
                    emit_av_ln(qt - 1)
            emit_av_ln(7)
        for c in range(n_chunks):
            nt, own0 = CHUNKS[c]
            all_tiles = list(range(nt))
            own_tiles = list(range(own0, own0 + 8))
            barrier()
            cap1 = 512 if c == 0 else CAP
            ws1 = "f32wb" if c == 0 else "bf16"
            pre1 = prefetch_e0(c) if ws1 == "bf16" else None
            lnr_phase(c, all_tiles, f"c{c}r1", cap=cap1,
                      xw=(list(x_marks) if c > 0 else [16] * 5))
            barrier()
            gb1 = routed_ffn_phase(c, all_tiles, f"c{c}f1", cap=cap1, wsrc=ws1,
                                   pre=pre1, defer_gather=do_attn)
            if do_attn:
                attn_phase(c, f"c{c}a", gb=gb1)
            if do_ffn2:
                barrier()
                pre2 = prefetch_e0(c)
                lnr_phase(c, own_tiles, f"c{c}r2", cap=CAP)
                barrier()
                routed_ffn_phase(c, own_tiles, f"c{c}f2", cap=CAP, wsrc="bf16", pre=pre2)
            wait("sync", "sV", cnt["sV"])
            stores += 1
            op("sync", lambda E, own0=own0, c=c: E.dma_start(
                out=out[c * 1024:(c + 1) * 1024, :].rearrange("(t p) n -> p t n", p=128),
                in_=x_sb.ap().rearrange("p (t n) -> p t n", t=NT_MAX)[:, own0:own0 + 8, :]),
                "sOut", 16)
            if c + 1 < n_chunks:
                wait("sync", "sOut", 16 * stores)
                g0 = (10 + 8 * c) * 128
                x_marks.clear()
                for part in range(4):
                    wait("sync", "sX", 16 * x_loads)
                    x_loads += 1
                    v = op("sync", lambda E, g0=g0, part=part: E.dma_start(
                        out=x_sb.ap().rearrange("p (t n) -> p t n", t=NT_MAX)[:, 2 * part:2 * part + 2, :],
                        in_=x_in[g0 + 256 * part:g0 + 256 * (part + 1), :].rearrange(
                            "(t p) n -> p t n", p=128)), "sX", 16)
                    x_marks.append(v)
        wait("sync", "sOut", 16 * stores)

        with nc.Block() as block:
            @block.tensor
            def _(E):
                for fn in ops["tensor"]:
                    fn(E)

            @block.vector
            def _(E):
                for fn in ops["vector"]:
                    fn(E)

            @block.scalar
            def _(E):
                for fn in ops["scalar"]:
                    fn(E)

            @block.gpsimd
            def _(E):
                for fn in ops["gpsimd"]:
                    fn(E)

            @block.sync
            def _(E):
                for fn in ops["sync"]:
                    fn(E)

    return nc


def make_masks(h):
    MU = np.triu(np.ones((128, 128), dtype=np.float32))
    ML = np.tril(np.ones((128, 128), dtype=np.float32), -1)
    ONES = np.ones((128, 128), dtype=np.float32)
    Z = np.zeros((128, 128), dtype=np.float32)
    if h == 1:
        MA, MB, MC = ML, ONES, ML
    else:
        MA, MB, MC = Z, Z, Z
    return np.stack([MU, ML, MA, MB, MC])


def kernel(L0_attn_x, L0_attn_W_qkv, L0_attn_b_qkv, L0_attn_ln_w, L0_attn_ln_b,
           L0_ffn_Wr, L0_ffn_br, L0_ffn_W1, L0_ffn_b1, L0_ffn_W2, L0_ffn_b2):
    x = np.ascontiguousarray(np.asarray(L0_attn_x, dtype=np.float32))
    wqkv = np.ascontiguousarray(np.asarray(L0_attn_W_qkv, dtype=np.float32))
    bqkv = np.ascontiguousarray(np.asarray(L0_attn_b_qkv, dtype=np.float32))
    lnw = np.ascontiguousarray(np.asarray(L0_attn_ln_w, dtype=np.float32))
    lnb = np.ascontiguousarray(np.asarray(L0_attn_ln_b, dtype=np.float32))
    wr = np.ascontiguousarray(np.asarray(L0_ffn_Wr, dtype=np.float32))
    br = np.ascontiguousarray(np.asarray(L0_ffn_br, dtype=np.float32))
    w1 = np.ascontiguousarray(np.asarray(L0_ffn_W1, dtype=np.float32))
    b1 = np.ascontiguousarray(np.asarray(L0_ffn_b1, dtype=np.float32))
    w2 = np.ascontiguousarray(np.asarray(L0_ffn_W2, dtype=np.float32))
    b2 = np.ascontiguousarray(np.asarray(L0_ffn_b2, dtype=np.float32))

    B, T, _ = x.shape
    idf = np.eye(128, dtype=np.float32)
    triv = np.zeros((128, 136), dtype=ml_dtypes.bfloat16)
    triv[:, 0:128] = np.triu(np.ones((128, 128), np.float32)).astype(ml_dtypes.bfloat16)
    triv[:, 128] = 1
    ecapv = np.stack([np.arange(8, dtype=np.float32) * 512,
                      np.arange(8, dtype=np.float32) * 256], axis=1)
    ones = np.ones((1, 512), dtype=np.float32)

    in_maps = []
    for c in range(8):
        b, h = c // 2, c % 2
        start = h * OWN
        if h == 0:
            xc = np.concatenate([np.zeros((HALO, D), np.float32),
                                 x[b, 0:OWN]], axis=0)
        else:
            xc = x[b, start - HALO:start + OWN]
        in_maps.append({
            "x_in": np.ascontiguousarray(xc),
            "w1": w1, "w2": w2, "wqkv": wqkv, "wr": wr, "br": br,
            "b1": b1, "b2": b2, "bqkv": bqkv, "lnw": lnw, "lnb": lnb,
            "idf": idf, "masks": make_masks(h), "ones_in": ones,
            "tri": triv, "ecap": ecapv,
        })
    nc = build_program()
    res = run_bass_kernel_spmd(nc, in_maps, core_ids=list(range(8)))
    outp = np.empty((B, T, D), dtype=np.float32)
    for c in range(8):
        b, h = c // 2, c % 2
        outp[b, h * OWN:(h + 1) * OWN] = res.results[c]["out"]
    return outp


if __name__ == "__main__":
    nc = build_program()
    n = sum(1 for _ in nc.all_instructions())
    print("built ok, instructions:", n)


# version 20 (final)
# speedup vs baseline: 1.0140x; 1.0140x over previous
import contextlib
import numpy as np
import ml_dtypes
import concourse.bass as bass
import concourse.mybir as mybir
from concourse.bass_utils import run_bass_kernel_spmd

F32 = mybir.dt.float32
BF16 = mybir.dt.bfloat16
AF = mybir.ActivationFunctionType
ALU = mybir.AluOpType
AX = mybir.AxisListType

D = 1024
KT = 8
DFF = 4096
NE = 8
OWN = 4096
HALO = 256
TOT = OWN + HALO
NT_MAX = 10
CW = NT_MAX * 128
LN_EPS = 1e-5
INV_SCALE = 1.0 / 512.0
NDM = 16
W1BLK = 512
NBLK = 2048 // W1BLK
BDM = W1BLK // 128
V_OFF = KT * CW

MASK_MU, MASK_ML, MASK_ONES, MASK_A, MASK_B, MASK_C = range(6)

CAP = 256
NST = CAP // 128

CHUNKS = [(10, 2), (8, 0), (8, 0), (8, 0)]


def build_program(n_chunks=4, n_experts=NE, do_attn=True, do_ffn2=True):
    nc = bass.Bass("TRN2", target_bir_lowering=False)

    x_in = nc.dram_tensor("x_in", [TOT, D], F32, kind="ExternalInput").ap()
    w1 = nc.dram_tensor("w1", [NE, D, DFF], F32, kind="ExternalInput").ap()
    w2 = nc.dram_tensor("w2", [NE, DFF, D], F32, kind="ExternalInput").ap()
    wqkv = nc.dram_tensor("wqkv", [D, 3 * D], F32, kind="ExternalInput").ap()
    wr = nc.dram_tensor("wr", [D, NE], F32, kind="ExternalInput").ap()
    br = nc.dram_tensor("br", [NE], F32, kind="ExternalInput").ap()
    b1 = nc.dram_tensor("b1", [NE, DFF], F32, kind="ExternalInput").ap()
    b2 = nc.dram_tensor("b2", [NE, D], F32, kind="ExternalInput").ap()
    bqkv = nc.dram_tensor("bqkv", [3 * D], F32, kind="ExternalInput").ap()
    lnw = nc.dram_tensor("lnw", [D], F32, kind="ExternalInput").ap()
    lnb = nc.dram_tensor("lnb", [D], F32, kind="ExternalInput").ap()
    idf = nc.dram_tensor("idf", [128, 128], F32, kind="ExternalInput").ap()
    masks = nc.dram_tensor("masks", [5, 128, 128], F32, kind="ExternalInput").ap()
    ones_in = nc.dram_tensor("ones_in", [1, 512], F32, kind="ExternalInput").ap()
    tri = nc.dram_tensor("tri", [128, 136], BF16, kind="ExternalInput").ap()
    ecap = nc.dram_tensor("ecap", [8, 2], F32, kind="ExternalInput").ap()
    out = nc.dram_tensor("out", [OWN, D], F32, kind="ExternalOutput").ap()
    w1s = nc.dram_tensor("w1s", [NE, 16 // BDM * 2, 128, KT * W1BLK], BF16, kind="Internal").ap()
    w2s = nc.dram_tensor("w2s", [NE, 2, 128, NDM * 1024], BF16, kind="Internal").ap()
    wqs = nc.dram_tensor("wqs", [6, 128, KT * 512], BF16, kind="Internal").ap()
    xg = nc.dram_tensor("xg", [NE * 512, D], BF16, kind="Internal").ap()
    yg = nc.dram_tensor("yg", [NE * 512, D], BF16, kind="Internal").ap()
    kvt = nc.dram_tensor("kvt", [128, KT * 256 + 2 * D], BF16, kind="Internal").ap()

    st = contextlib.ExitStack()
    with st:
        sbt = lambda name, shape, dt: st.enter_context(nc.sbuf_tensor(name, shape, dt))
        x_sb = sbt("x_sb", [128, NT_MAX * D], F32)
        xnT = sbt("xnT", [128, KT * CW], BF16)
        bigA = sbt("bigA", [128, NDM * CW], BF16)
        yacc = sbt("yacc", [128, NT_MAX * D], F32)
        w2h = sbt("w2h", [128, NDM * D], BF16)
        w1blk = sbt("w1blk", [128, 2 * KT * W1BLK], BF16)
        xn_bf = sbt("xn_bf", [128, D], F32)
        b1_sb = sbt("b1_sb", [128, NE * 32], F32)
        b2_sb = sbt("b2_sb", [NE, D], BF16)
        bqkvT = sbt("bqkvT", [128, 16], F32)
        wr_sb = sbt("wr_sb", [128, KT * NE], BF16)
        br_row = sbt("br_row", [1, NE], BF16)
        ones_bf = sbt("ones_bf", [1, 128], BF16)
        lnwb = sbt("lnwb", [128, D], F32)
        lnbb = sbt("lnbb", [128, D], F32)
        id_f = sbt("id_f", [128, 128], F32)
        mk_sb = sbt("mk_sb", [128, 5 * 128], BF16)
        S_sb = sbt("S_sb", [128, NT_MAX * NE], F32)
        mskT = sbt("mskT", [8, 2 * 128], BF16)
        stats = sbt("stats", [128, 68], F32)
        tri_sb = sbt("tri_sb", [128, 136], BF16)

        Mbf = sbt("Mbf", [128, 16], BF16)
        bcol = sbt("bcol", [8, 8], F32)
        dint = sbt("dint", [128, 12], mybir.dt.uint32)
        prefs = sbt("prefs", [8, 256], F32).ap()
        Ssel_sb = sbt("Ssel_sb", [128, 12], F32)
        ptk = sbt("ptk", [128, 16], F32)

        P = [st.enter_context(nc.psum_tensor(f"P{i}", [128, 1024], F32)) for i in range(4)]

        sem_names = ["sW1a", "sW1b", "sW2", "sQ", "sX", "sCg", "sCs", "sOut",
                     "sT", "sV", "sA", "sWBa", "sWBb", "sWB2", "sWBq",
                     "sSC", "sYGa", "sYGb", "sGB", "sXG", "sW2h", "sW2g",
                     "sW2i", "sW2j", "sSC2", "sKV", "sGB2", "sW1c", "sW1d"]
        SEM = {n: st.enter_context(nc.semaphore(n)) for n in sem_names}

        ops = {"tensor": [], "scalar": [], "vector": [], "gpsimd": [], "sync": []}
        cnt = {n: 0 for n in sem_names}
        last_wait = {}

        def wait(eng, sem, val):
            if val <= 0:
                return
            k = (eng, sem)
            if last_wait.get(k, 0) >= val:
                return
            last_wait[k] = val
            ops[eng].append(lambda E, s=sem, v=val: E.wait_ge(SEM[s], v))

        def inc(sem, n=1):
            cnt[sem] += n
            return cnt[sem]

        def op(eng, fn, sem=None, n=1):
            if sem is None:
                ops[eng].append(fn)
                return None
            v = inc(sem, n)
            ops[eng].append(lambda E, f=fn, s=sem, k=n: f(E).then_inc(SEM[s], k))
            return v

        def drain(eng):
            ops[eng].append(lambda E: E.drain())

        def barrier(consumers=("tensor", "scalar", "vector")):
            for eng in consumers:
                for s in ("sT", "sV", "sA"):
                    wait(eng, s, cnt[s])

        bank_free = [("sV", 0)] * 8

        def bank_wait(eng, banks):
            for b in banks:
                s, v = bank_free[b]
                wait(eng, s, v)

        def bank_release(banks, sem, val):
            for b in banks:
                bank_free[b] = (sem, val)

        x_t = lambda i: x_sb[:, i * D:(i + 1) * D]
        yacc_t = lambda i: yacc[:, i * D:(i + 1) * D]
        S_t = lambda i: S_sb[:, i * NE:(i + 1) * NE]
        xnT_k = lambda k, a, b: xnT[:, k * CW + a: k * CW + b]
        h1T_t = lambda dm, a, b: bigA[:, dm * CW + a: dm * CW + b]
        kT_k = lambda k, a, b: bigA[:, k * CW + a: k * CW + b]
        v_t = lambda i, a, b: bigA[:, V_OFF + i * D + a: V_OFF + i * D + b]
        w2h_k = lambda dk, a, b: w2h[:, dk * D + a: dk * D + b]
        rh1T = lambda dm, a, b, cap: bigA[:, dm * cap + a: dm * cap + b]
        rxgT = lambda k, a, b, cap: bigA[:, 16 * cap + k * cap + a:
                                         16 * cap + k * cap + b]
        rge_v = lambda cap, sl=0: bigA[:, 24 * cap + (0 if sl == 0 else 3 * D):
                                       24 * cap + (0 if sl == 0 else 3 * D) + D]
        ryst = lambda s, cap: bigA[:, 24 * cap + D + s * D:
                                   24 * cap + D + (s + 1) * D]
        rb2row = lambda cap: bigA[0:1, 24 * cap + 4 * D:24 * cap + 5 * D]
        W1SL = KT * W1BLK

        def w1v(slot, k, a, b, cap):
            if slot < 2:
                base = slot * W1SL
                return w1blk[:, base + k * W1BLK + a: base + k * W1BLK + b]
            base = 24 * cap + 5 * D + (slot - 2) * W1SL
            return bigA[:, base + k * W1BLK + a: base + k * W1BLK + b]

        def w1v_full(slot, cap):
            if slot < 2:
                return w1blk[:, slot * W1SL:(slot + 1) * W1SL]
            base = 24 * cap + 5 * D + (slot - 2) * W1SL
            return bigA[:, base:base + W1SL]
        bv_row = w2h[0:1, 13056:14080]
        qT_k = lambda k, a, b: w2h[:, k * D + a: k * D + b]
        wqb_k = lambda k, a, b: w2h[:, 8192 + k * 512 + a: 8192 + k * 512 + b]
        scob_s = lambda s, a, b: w2h[:, 12288 + s * 384 + a: 12288 + s * 384 + b]
        w1b_k = lambda slot, k, a, b: w1blk[:, slot * KT * W1BLK + k * W1BLK + a:
                                            slot * KT * W1BLK + k * W1BLK + b]

        def pbank(pi, half):
            return 2 * pi + half

        op("sync", lambda E: E.dma_start(out=id_f[:], in_=idf[:]), "sCs", 16)
        lnrow = yacc[0:1, 0:2 * D]
        onesf = yacc[0:1, 2 * D:2 * D + 512]
        b1_stage = yacc[0:8, 4096:4096 + DFF]
        bq_stage = yacc[0:16, 2 * D + 512:2 * D + 512 + 128]
        op("sync", lambda E: E.dma_start(out=lnrow[:, 0:D], in_=lnw[None, :]), "sCs", 16)
        op("sync", lambda E: E.dma_start(out=lnrow[:, D:2 * D], in_=lnb[None, :]), "sCs", 16)
        op("sync", lambda E: E.dma_start(out=onesf, in_=ones_in[:]), "sCs", 16)
        op("sync", lambda E: E.dma_start(out=b1_stage, in_=b1[:, :]), "sCs", 16)
        op("sync", lambda E: E.dma_start(
            out=bq_stage, in_=bqkv[0:2 * D].rearrange("(t p) -> t p", p=128)), "sCs", 16)
        op("sync", lambda E: E.dma_start(
            out=x_sb.ap().rearrange("p (t n) -> p t n", t=NT_MAX),
            in_=x_in[0:NT_MAX * 128, :].rearrange("(t p) n -> p t n", p=128)), "sX", 16)
        op("gpsimd", lambda E: E.dma_start(
            out=mk_sb.ap().rearrange("p (m n) -> p m n", m=5),
            in_=masks.rearrange("m p n -> p m n")), "sCg", 16)
        op("gpsimd", lambda E: E.dma_start(
            out=wr_sb.ap().rearrange("p (k e) -> p k e", k=KT),
            in_=wr.rearrange("(k p) e -> p k e", p=128)), "sCg", 16)
        op("gpsimd", lambda E: E.dma_start(out=br_row[:], in_=br[None, :]), "sCg", 16)
        op("gpsimd", lambda E: E.dma_start(out=ones_bf[:], in_=ones_in[0:1, 0:128]), "sCg", 16)
        op("gpsimd", lambda E: E.dma_start(out=b2_sb[:], in_=b2[:, :]), "sCg", 16)
        op("gpsimd", lambda E: E.dma_start(out=tri_sb[:], in_=tri[:]), "sCg", 16)
        op("sync", lambda E: E.dma_start(out=bcol[:, 2:4], in_=ecap[:]), "sCs", 16)

        for e in ("tensor", "vector", "scalar"):
            wait(e, "sCs", cnt["sCs"])
            wait(e, "sCg", cnt["sCg"])
        wait("tensor", "sX", cnt["sX"])
        epsv = op("vector", lambda E: E.memset(stats[:, 64:65], LN_EPS), "sV")
        wait("scalar", "sV", epsv)
        op("vector", lambda E: E.tensor_scalar(
            bcol[:, 4:5], bcol[:, 2:3], 511.0, None, op0=ALU.add))
        op("vector", lambda E: E.tensor_scalar(
            bcol[:, 5:6], bcol[:, 3:4], float(CAP - 1), None, op0=ALU.add))
        zv = op("vector", lambda E: E.memset(rge_v(384, 0), 0.0), "sV")
        wait("gpsimd", "sV", zv)
        for zi in range(NE * 512 // 128):
            op("gpsimd", lambda E, zi=zi: E.dma_start(
                out=xg[zi * 128:(zi + 1) * 128, :], in_=rge_v(384, 0)), "sSC", 16)
        wait("scalar", "sSC", cnt["sSC"])
        wait("vector", "sSC", cnt["sSC"])

        for dm in range(32):
            ps = P[dm % 2]
            bank_wait("tensor", [pbank(dm % 2, 0)])
            tv = op("tensor", lambda E, dm=dm, ps=ps: E.transpose(
                ps[:, 0:8], b1_stage[:, dm * 128:(dm + 1) * 128], id_f[0:8, 0:8]), "sT")
            wait("vector", "sT", tv)
            dst = b1_sb.ap().rearrange("p (e m) -> p e m", e=NE)[:, :, dm]
            vv = op("vector", lambda E, d=dst, ps=ps: E.tensor_copy(d, ps[:, 0:8]), "sV")
            bank_release([pbank(dm % 2, 0)], "sV", vv)
        bank_wait("tensor", [4])
        tv = op("tensor", lambda E: E.transpose(P[2][:, 0:16], bq_stage, id_f[0:16, 0:16]), "sT")
        wait("vector", "sT", tv)
        op("vector", lambda E: E.tensor_scalar(bqkvT[:, 0:8], P[2][:, 0:8],
                                               INV_SCALE, None, op0=ALU.mult))
        vv = op("vector", lambda E: E.tensor_copy(bqkvT[:, 8:16], P[2][:, 8:16]), "sV")
        bank_release([4], "sV", vv)
        bank_wait("tensor", [6, 7])
        for half in range(2):
            op("tensor", lambda E, h=half: E.matmul(
                P[3][:, 512 * h:512 * (h + 1)], onesf[:, 0:128],
                lnrow[:, 512 * h:512 * (h + 1)], start=True, stop=True))
        bank_wait("tensor", [4, 5])
        tv = op("tensor", lambda E: E.matmul(
            P[2][:, 0:512], onesf[:, 0:128], lnrow[:, 1024:1536], start=True, stop=True), "sT")
        tv2 = op("tensor", lambda E: E.matmul(
            P[2][:, 512:1024], onesf[:, 0:128], lnrow[:, 1536:2048], start=True, stop=True), "sT")
        wait("vector", "sT", tv2)
        op("vector", lambda E: E.tensor_copy(lnwb[:], P[3][:]))
        vv = op("vector", lambda E: E.tensor_copy(lnbb[:], P[2][:]), "sV")
        bank_release([4, 5, 6, 7], "sV", vv)
        setup_done_v = cnt["sV"]
        setup_cg = cnt["sCg"]

        w1_sems = ["sW1a", "sW1b"]
        w1_slot_free = [[("sV", 0)], [("sV", 0)]]
        w1_wb_tail = [("sWBa", 0), ("sWBb", 0)]
        x_marks = []
        xgT_free = [0]
        w2h_loads = [0]
        b2row_free = [0]
        w1_sems4 = ["sW1a", "sW1b", "sW1c", "sW1d"]
        w1_slot_free4 = [("sV", 0)] * 4
        w2lo_free = [0]
        yst_free = [("sYGa", 0), ("sYGb", 0)]
        w2_wb_tail = []
        wq_wb_tail = []
        w1_loads = [0, 0]
        w2_loads = 0
        q_loads = 0
        x_loads = 1
        stores = 0

        def ln_phase(c, tiles, tag, pre_tile=None):
            tr_slot = [0, 0]
            for ti, t in enumerate(tiles):
                if pre_tile is not None:
                    pre_tile(ti, t)
                    wait("scalar", "sV", cnt["sV"])
                    drain("vector")
                s = (t % 2) * 32
                op("vector", lambda E, t=t, s=s: E.reduce_sum(
                    stats[:, s:s + 1], x_t(t), axis=AX.X))
                av = op("scalar", lambda E, t=t: E.activation(
                    yacc_t(t), x_t(t), AF.Square), "sA")
                wait("vector", "sA", av)
                op("vector", lambda E, t=t, s=s: E.reduce_sum(
                    stats[:, s + 1:s + 2], yacc_t(t), axis=AX.X))
                drain("vector")
                op("vector", lambda E, s=s: E.tensor_scalar(
                    stats[:, s + 2:s + 3], stats[:, s:s + 1], 1.0 / D, None, op0=ALU.mult))
                drain("vector")
                vv = op("vector", lambda E, s=s: E.scalar_tensor_tensor(
                    stats[:, s + 3:s + 4], stats[:, s:s + 1], stats[:, s + 2:s + 3],
                    stats[:, s + 1:s + 2], op0=ALU.mult, op1=ALU.subtract), "sV")
                wait("scalar", "sV", vv)
                av = op("scalar", lambda E, s=s: E.activation(
                    stats[:, s + 4:s + 5], stats[:, s + 3:s + 4], AF.Sqrt,
                    bias=stats[:, 64:65], scale=-1.0 / D), "sA")
                wait("vector", "sA", av)
                op("vector", lambda E, s=s: E.reciprocal(
                    stats[:, s + 5:s + 6], stats[:, s + 4:s + 5]))
                drain("vector")
                if c != 0 and len(tiles) <= 8:
                    xnbuf = xn_bf[:] if ti % 2 == 0 else yacc_t(8)
                    wait("vector", "sT", tr_slot[ti % 2])
                else:
                    xnbuf = xn_bf[:]
                    wait("vector", "sT", tr_slot[0])
                vv = op("vector", lambda E, t=t, s=s, xb=xnbuf: E.tensor_scalar(
                    xb, x_t(t), stats[:, s + 2:s + 3], stats[:, s + 5:s + 6],
                    op0=ALU.subtract, op1=ALU.mult), "sV")
                wait("tensor", "sV", vv)
                last_t = None
                for k in range(KT):
                    pi = (t * KT + k) % 2
                    bank_wait("tensor", [pbank(pi, 0)])
                    tvv = op("tensor", lambda E, k=k, pi=pi, xb=xnbuf: E.transpose(
                        P[pi][:, 0:128], xb[:, k * 128:(k + 1) * 128], id_f[:]), "sT")
                    wait("vector", "sT", tvv)
                    cv = op("vector", lambda E, k=k, t=t, pi=pi: E.tensor_copy(
                        xnT_k(k, t * 128, (t + 1) * 128), P[pi][:, 0:128]), "sV")
                    bank_release([pbank(pi, 0)], "sV", cv)
                    last_t = tvv
                if c != 0 and len(tiles) <= 8:
                    tr_slot[ti % 2] = last_t
                else:
                    tr_slot[0] = last_t

        def lnr_phase(c, tiles, tag, cap, xw=None):
            ecap_c = 2 if cap == 512 else 3
            bound_c = 4 if cap == 512 else 5
            routed = True
            tr_slot = [0, 0]
            sc_slot = [("sSC", 0), ("sSC", 0)]
            bz = op("vector", lambda E: E.memset(bcol[:, 0:1], 0.0), "sV")
            drain("vector")
            for ti, t in enumerate(tiles):
                s = (t % 2) * 32
                if xw is not None and t // 2 < len(xw):
                    wait("vector", "sX", xw[t // 2])
                    wait("scalar", "sX", xw[t // 2])
                op("vector", lambda E, t=t, s=s: E.reduce_sum(
                    stats[:, s:s + 1], x_t(t), axis=AX.X))
                av = op("scalar", lambda E, t=t: E.activation(
                    yacc_t(t), x_t(t), AF.Square), "sA")
                wait("vector", "sA", av)
                op("vector", lambda E, t=t, s=s: E.reduce_sum(
                    stats[:, s + 1:s + 2], yacc_t(t), axis=AX.X))
                drain("vector")
                op("vector", lambda E, s=s: E.tensor_scalar(
                    stats[:, s + 2:s + 3], stats[:, s:s + 1], 1.0 / D, None, op0=ALU.mult))
                drain("vector")
                vv = op("vector", lambda E, s=s: E.scalar_tensor_tensor(
                    stats[:, s + 3:s + 4], stats[:, s:s + 1], stats[:, s + 2:s + 3],
                    stats[:, s + 1:s + 2], op0=ALU.mult, op1=ALU.subtract), "sV")
                wait("scalar", "sV", vv)
                av = op("scalar", lambda E, s=s: E.activation(
                    stats[:, s + 4:s + 5], stats[:, s + 3:s + 4], AF.Sqrt,
                    bias=stats[:, 64:65], scale=-1.0 / D), "sA")
                wait("vector", "sA", av)
                op("vector", lambda E, s=s: E.reciprocal(
                    stats[:, s + 5:s + 6], stats[:, s + 4:s + 5]))
                drain("vector")
                pingpong = len(tiles) <= 8
                slotB = 8 if tiles[0] == 0 else 0
                if pingpong:
                    xslot = ti % 2
                    xnbuf = xn_bf[:] if xslot == 0 else yacc_t(slotB)
                    wait("vector", "sT", tr_slot[xslot])
                else:
                    xslot = 0
                    xnbuf = xn_bf[:]
                    wait("vector", "sT", tr_slot[0])
                wait("vector", sc_slot[xslot][0], sc_slot[xslot][1])
                vv = op("vector", lambda E, t=t, s=s, xb=xnbuf: E.tensor_scalar(
                    xb, x_t(t), stats[:, s + 2:s + 3], stats[:, s + 5:s + 6],
                    op0=ALU.subtract, op1=ALU.mult), "sV")
                wait("tensor", "sV", vv)
                last_t = None
                for k in range(KT):
                    pi = (t * KT + k) % 2
                    bank_wait("tensor", [pbank(pi, 0)])
                    tvv = op("tensor", lambda E, k=k, pi=pi, xb=xnbuf: E.transpose(
                        P[pi][:, 0:128], xb[:, k * 128:(k + 1) * 128], id_f[:]), "sT")
                    wait("vector", "sT", tvv)
                    cv = op("vector", lambda E, k=k, t=t, pi=pi: E.tensor_copy(
                        xnT_k(k, t * 128, (t + 1) * 128), P[pi][:, 0:128]), "sV")
                    bank_release([pbank(pi, 0)], "sV", cv)
                    last_t = tvv
                if pingpong:
                    tr_slot[ti % 2] = last_t
                else:
                    tr_slot[0] = last_t
                s = (t % 2) * 32
                half = ti % 2
                lg_col = 512 * half
                bank_wait("tensor", [4 + half])
                wait("tensor", "sV", cv)
                for k in range(KT):
                    op("tensor", lambda E, k=k, t=t, lc=lg_col: E.matmul(
                        P[2][:, lc:lc + 8], xnT_k(k, t * 128, (t + 1) * 128),
                        wr_sb[:, k * 8:(k + 1) * 8], start=(k == 0), stop=False))
                tv = op("tensor", lambda E, lc=lg_col: E.matmul(
                    P[2][:, lc:lc + 8], ones_bf[0:1, 0:128], br_row[:],
                    start=False, stop=True), "sT")
                wait("vector", "sT", tv)
                vv = op("vector", lambda E, s=s, lc=lg_col: E.tensor_copy(
                    stats[:, s + 8:s + 16], P[2][:, lc:lc + 8]), "sV")
                wait("scalar", "sV", vv)
                av = op("scalar", lambda E, s=s: E.activation(
                    stats[:, s + 16:s + 24], stats[:, s + 8:s + 16], AF.Exp), "sA")
                drain("vector")
                op("vector", lambda E, s=s: E.reduce_max(
                    stats[:, s:s + 1], stats[:, s + 8:s + 16], axis=AX.X))
                wait("vector", "sA", av)
                op("vector", lambda E, s=s: E.reduce_sum(
                    stats[:, s + 2:s + 3], stats[:, s + 16:s + 24], axis=AX.X))
                drain("vector")
                op("vector", lambda E, s=s: E.reciprocal(
                    stats[:, s + 3:s + 4], stats[:, s + 2:s + 3]))
                op("vector", lambda E, s=s: E.tensor_scalar(
                    stats[:, s + 24:s + 32], stats[:, s + 8:s + 16], stats[:, s:s + 1],
                    None, op0=ALU.is_equal))
                drain("vector")
                op("vector", lambda E, s=s: E.tensor_tensor(
                    stats[:, s + 16:s + 24], stats[:, s + 16:s + 24],
                    stats[:, s + 24:s + 32], op=ALU.mult))
                drain("vector")
                lv = op("vector", lambda E, t=t, s=s: E.tensor_scalar(
                    S_t(t), stats[:, s + 16:s + 24], stats[:, s + 3:s + 4],
                    0.5, op0=ALU.mult, op1=ALU.mult), "sV")
                wait("tensor", "sV", lv)
                if routed:
                    ms = ti % 2
                    mv2 = op("vector", lambda E, s=s, ms=ms: E.tensor_copy(
                        Mbf[:, ms * 8:(ms + 1) * 8], stats[:, s + 24:s + 32]), "sV")
                    wait("tensor", "sV", mv2)
                    wait("tensor", "sCg", setup_cg)
                    tv2 = op("tensor", lambda E, ms=ms, lc=lg_col: E.matmul(
                        P[2][0:8, lc + 160:lc + 296], Mbf[:, ms * 8:(ms + 1) * 8],
                        tri_sb[:], start=True, stop=True), "sT")
                    wait("vector", "sT", tv2)
                    op("vector", lambda E: E.scalar_tensor_tensor(
                        bcol[:, 1:2], bcol[:, 0:1], -1.0, bcol[:, ecap_c:ecap_c + 1],
                        op0=ALU.add, op1=ALU.add))
                    drain("vector")
                    op("vector", lambda E, ms=ms, lc=lg_col: E.tensor_scalar(
                        prefs[:, ms * 128:(ms + 1) * 128],
                        P[2][0:8, lc + 160:lc + 288], bcol[:, 1:2], None, op0=ALU.add))
                    drain("vector")
                    vv3 = op("vector", lambda E, ms=ms: E.tensor_scalar(
                        prefs[:, ms * 128:(ms + 1) * 128],
                        prefs[:, ms * 128:(ms + 1) * 128], bcol[:, bound_c:bound_c + 1], None, op0=ALU.min), "sV")
                    op("vector", lambda E, lc=lg_col: E.tensor_scalar(
                        bcol[:, 0:1], P[2][0:8, lc + 288:lc + 289], bcol[:, 0:1],
                        None, op0=ALU.add))
                    drain("vector")
                    wait("tensor", "sV", vv3)
                    tv3 = op("tensor", lambda E, ms=ms, lc=lg_col: E.transpose(
                        P[2][:, lc + 304:lc + 312], prefs[:, ms * 128:(ms + 1) * 128],
                        id_f[0:8, 0:8]), "sT")
                    wait("vector", "sT", tv3)
                    op("vector", lambda E, s=s, ms=ms, lc=lg_col: E.tensor_tensor(
                        ptk[:, ms * 8:(ms + 1) * 8], P[2][:, lc + 304:lc + 312],
                        stats[:, s + 24:s + 32], op=ALU.mult))
                    drain("vector")
                    op("vector", lambda E, s=s, ms=ms: E.reduce_sum(
                        stats[:, s + 6:s + 7], ptk[:, ms * 8:(ms + 1) * 8], axis=AX.X))
                    drain("vector")
                    op("vector", lambda E, s=s, ti=ti: E.tensor_copy(
                        dint[:, ti:ti + 1], stats[:, s + 6:s + 7]))
                    dv2 = op("vector", lambda E, t=t, ti=ti: E.reduce_sum(
                        Ssel_sb[:, ti:ti + 1], S_t(t), axis=AX.X), "sV")
                    bank_release([4 + half], "sV", dv2)
                    wait("gpsimd", "sV", dv2)
                    ssem = "sSC" if ti % 2 == 0 else "sSC2"
                    scv = op("gpsimd", lambda E, ti=ti, xb=xnbuf: E.indirect_dma_start(
                        out=xg, out_offset=bass.IndirectOffsetOnAxis(ap=dint[:, ti:ti + 1], axis=0),
                        in_=xb, in_offset=None), ssem, 16)
                    sc_slot[xslot] = (ssem, scv)


        def prefetch_e0(c):
            pre = {}
            blks = []
            for blk in range(2):
                slot = blk
                ssem = w1_sems[slot]
                for fs, fv in w1_slot_free[slot]:
                    wait("gpsimd", fs, fv)
                ldv = inc(ssem, 16)
                blks.append((ssem, ldv))
                ops["gpsimd"].append(
                    lambda E, bg=blk, slot=slot, ss=ssem: E.dma_start(
                        out=w1blk[:, slot * KT * W1BLK:(slot + 1) * KT * W1BLK],
                        in_=w1s[0, bg]
                    ).then_inc(SEM[ss], 16))
            wait("gpsimd", "sT", cnt["sT"])
            for blk in range(2, 4):
                ssem = w1_sems4[blk]
                ldv = inc(ssem, 16)
                blks.append((ssem, ldv))
                ops["gpsimd"].append(
                    lambda E, bg=blk, slot=blk, ss=ssem: E.dma_start(
                        out=w1v_full(slot, CAP),
                        in_=w1s[0, bg]
                    ).then_inc(SEM[ss], 16))
            pre["blks"] = blks
            wait("scalar", "sT", cnt["sT"])
            for fs, fv in w2_wb_tail:
                wait("scalar", fs, fv)
            for fs, fv in wq_wb_tail:
                wait("scalar", fs, fv)
            w2q = {}
            for j in range(2):
                wsem = "sW2h" if j == 0 else "sW2g"
                w2q["lo" if j == 0 else "hi"] = (wsem, inc(wsem, 16))
                ops["scalar"].append(
                    lambda E, j=j, wsem=wsem: E.dma_start(
                        out=w2h[:, j * 8192:(j + 1) * 8192],
                        in_=w2s[0, 0][:, j * 8192:(j + 1) * 8192]
                    ).then_inc(SEM[wsem], 16))
            pre["w2q"] = w2q
            return pre

        def routed_ffn_phase(c, tiles, tag, cap=CAP, wsrc="bf16", pre=None,
                             defer_gather=False):
            nonlocal w2_loads, q_loads
            nst = cap // 128
            prs = [2, 3, 0, 1][:nst]
            wb_sems = {0: "sWBa", 1: "sWBb"}
            scat_done = cnt["sSC"]
            scat_done2 = cnt["sSC2"]
            if wsrc == "bf16":
                w1_slot_free4[2] = ("sT", cnt["sT"])
                w1_slot_free4[3] = ("sT", cnt["sT"])
            yga, ygb = 0, 0
            for e in range(n_experts):
                wait("scalar", "sSC", scat_done)
                wait("scalar", "sSC2", scat_done2)
                if xgT_free[0]:
                    wait("scalar", "sT", xgT_free[0])
                for dq in range(KT):
                    inc("sXG", 16)
                    ops["scalar"].append(lambda E, e=e, dq=dq, cap=cap: E.dma_start(
                        out=rxgT(dq, 0, cap, cap), in_=xg[e * cap:(e + 1) * cap, dq * 128:(dq + 1) * 128],
                        transpose=True).then_inc(SEM["sXG"], 16))
                xgq = cnt["sXG"]
                wait("tensor", "sXG", xgq)
                wait("gpsimd", "sT", b2row_free[0])
                b2v = op("gpsimd", lambda E, e=e, cap=cap: E.dma_start(
                    out=rb2row(cap), in_=b2[e:e + 1, :]), "sQ", 16)
                wait("tensor", "sQ", b2v)
                for h in range(2):
                    per_blk = BDM
                    nslot = 4 if wsrc == "bf16" else 2
                    sT0 = cnt["sT"]
                    blk_ld = []
                    for blk in range(NBLK):
                        slot = blk % nslot
                        ssem = w1_sems4[slot] if nslot == 4 else w1_sems[slot]
                        if e == 0 and h == 0 and pre is not None and blk < len(pre["blks"]):
                            blk_ld.append(pre["blks"][blk])
                            continue
                        if nslot == 4:
                            fs, fv = w1_slot_free4[slot]
                            wait("gpsimd", fs, fv)
                        elif blk >= 2:
                            wait("gpsimd", "sT", sT0 + (blk - 1) * per_blk)
                        else:
                            for fs, fv in w1_slot_free[slot]:
                                wait("gpsimd", fs, fv)
                        if wsrc == "f32wb" and blk >= 2:
                            fs, fv = w1_wb_tail[slot]
                            wait("gpsimd", fs, fv)
                        ldv = inc(ssem, 16)
                        blk_ld.append((ssem, ldv))
                        bg = h * NBLK + blk
                        if wsrc == "f32wb":
                            cblk = h * 2048 + blk * W1BLK
                            ops["gpsimd"].append(
                                lambda E, cb=cblk, slot=slot, ss=ssem, e=e: E.dma_start(
                                    out=w1blk.ap().rearrange("p (s k n) -> p s k n", s=2, k=KT)[:, slot],
                                    in_=w1[e].rearrange("(k p) n -> p k n", p=128)[:, :, cb:cb + W1BLK]
                                ).then_inc(SEM[ss], 16))
                            wbs = wb_sems[slot]
                            wait("sync", ssem, ldv)
                            wbv = inc(wbs, 16)
                            ops["sync"].append(
                                lambda E, bg=bg, slot=slot, wb=wbs, e=e: E.dma_start(
                                    out=w1s[e, bg],
                                    in_=w1blk[:, slot * KT * W1BLK:(slot + 1) * KT * W1BLK]
                                ).then_inc(SEM[wb], 16))
                            w1_wb_tail[slot] = (wbs, wbv)
                        else:
                            ops["gpsimd"].append(
                                lambda E, bg=bg, slot=slot, ss=ssem, e=e, cap=cap: E.dma_start(
                                    out=w1v_full(slot, cap),
                                    in_=w1s[e, bg]
                                ).then_inc(SEM[ss], 16))
                    if nslot == 4:
                        for blk in range(NBLK):
                            w1_slot_free4[blk % 4] = ("sT", sT0 + (blk + 1) * per_blk)
                        w1_slot_free[0] = [w1_slot_free4[0]]
                        w1_slot_free[1] = [w1_slot_free4[1]]
                    else:
                        w1_slot_free[0] = [("sT", sT0 + (NBLK - 1) * per_blk)] + \
                            ([w1_wb_tail[0]] if wsrc == "f32wb" else [])
                        w1_slot_free[1] = [("sT", sT0 + NBLK * per_blk)] + \
                            ([w1_wb_tail[1]] if wsrc == "f32wb" else [])
                        w1_slot_free4[0] = w1_slot_free[0][0]
                        w1_slot_free4[1] = w1_slot_free[1][0]
                    if e == 0 and h == 0 and pre is not None:
                        w2q = pre["w2q"]
                        skip_w2 = True
                    else:
                        skip_w2 = False
                    if not skip_w2 and wsrc == "f32wb":
                        wait("gpsimd", "sT", cnt["sT"])
                        for fs, fv in w2_wb_tail:
                            wait("gpsimd", fs, fv)
                        w2_wb_tail.clear()
                        w2src = w2[e].rearrange("(dk p) n -> p dk n", p=128)
                        w2q = {}
                        for j in range(4):
                            wsem = "sW2i" if j < 2 else "sW2j"
                            w2q["lo" if j < 2 else "hi"] = (wsem, inc(wsem, 16))
                            ops["gpsimd"].append(
                                lambda E, h=h, j=j, ws=w2src, wsem=wsem: E.dma_start(
                                    out=w2h.ap().rearrange("p (dk n) -> p dk n", dk=NDM)[:, 4 * j:4 * j + 4, :],
                                    in_=ws[:, h * NDM + 4 * j: h * NDM + 4 * j + 4, :]
                                ).then_inc(SEM[wsem], 16))
                        wait("sync", w2q["lo"][0], w2q["lo"][1])
                        wait("sync", w2q["hi"][0], w2q["hi"][1])
                        for j in range(2):
                            wbv = inc("sWB2", 16)
                            ops["sync"].append(
                                lambda E, h=h, j=j, e=e: E.dma_start(
                                    out=w2s[e, h][:, j * 8192:(j + 1) * 8192],
                                    in_=w2h[:, j * 8192:(j + 1) * 8192]
                                ).then_inc(SEM["sWB2"], 16))
                        w2_wb_tail.append(("sWB2", wbv))
                    elif not skip_w2:
                        for fs, fv in w2_wb_tail:
                            wait("scalar", fs, fv)
                        for fs, fv in wq_wb_tail:
                            wait("scalar", fs, fv)
                        w2q = {}
                        for j in range(2):
                            if j == 0 and w2lo_free[0]:
                                wait("scalar", "sT", w2lo_free[0])
                            if j == 1:
                                wait("scalar", "sT", cnt["sT"])
                            wsem = "sW2h" if j == 0 else "sW2g"
                            w2q["lo" if j == 0 else "hi"] = (wsem, inc(wsem, 16))
                            ops["scalar"].append(
                                lambda E, h=h, j=j, e=e, wsem=wsem: E.dma_start(
                                    out=w2h[:, j * 8192:(j + 1) * 8192],
                                    in_=w2s[e, h][:, j * 8192:(j + 1) * 8192]
                                ).then_inc(SEM[wsem], 16))
                    for blk in range(NBLK):
                        ssem, ldv = blk_ld[blk]
                        wait("tensor", ssem, ldv)
                        for dml in range(BDM):
                            dm = blk * BDM + dml
                            ps = dm % 4
                            pt, pcol = ps // 2, (ps % 2) * 512
                            bank_wait("tensor", [pbank(pt, ps % 2)])
                            for k in range(KT):
                                fn = lambda E, pt=pt, pcol=pcol, slot=blk % nslot, k=k, dml=dml, cap=cap: E.matmul(
                                    P[pt][:, pcol:pcol + cap],
                                    w1v(slot, k, dml * 128, dml * 128 + 128, cap),
                                    rxgT(k, 0, cap, cap), start=(k == 0), stop=(k == KT - 1))
                                if k == KT - 1:
                                    tv = op("tensor", fn, "sT")
                                else:
                                    op("tensor", fn)
                            wait("scalar", "sT", tv)
                            bcol_i = e * 32 + h * 16 + dm
                            av = op("scalar", lambda E, dm=dm, pt=pt, pcol=pcol, bc=bcol_i, cap=cap: E.activation(
                                rh1T(dm, 0, cap, cap), P[pt][:, pcol:pcol + cap],
                                AF.Relu, bias=b1_sb[:, bc:bc + 1], scale=1.0), "sA")
                            bank_release([pbank(pt, ps % 2)], "sA", av)
                    if h == 1:
                        xgT_free[0] = cnt["sT"]
                    h1_done = cnt["sA"]
                    wait("tensor", "sA", h1_done)
                    evs = []
                    for st in range(nst):
                        pi = prs[st]
                        bank_wait("tensor", [pbank(pi, 0), pbank(pi, 1)])
                        if st == 0:
                            wait("tensor", w2q["lo"][0], w2q["lo"][1])
                        for dk in range(NDM):
                            if st == 0 and dk == 8:
                                wait("tensor", w2q["hi"][0], w2q["hi"][1])
                            last = (dk == NDM - 1 and h == 0)
                            op("tensor", lambda E, pi=pi, dk=dk, st=st, cap=cap, sp=last: E.matmul(
                                P[pi][:, 0:512], rh1T(dk, st * 128, st * 128 + 128, cap),
                                w2h_k(dk, 0, 512), start=(dk == 0), stop=sp))
                            fn = lambda E, pi=pi, dk=dk, st=st, cap=cap, sp=last: E.matmul(
                                P[pi][:, 512:1024], rh1T(dk, st * 128, st * 128 + 128, cap),
                                w2h_k(dk, 512, 1024), start=(dk == 0), stop=sp)
                            if dk == NDM - 1 and h == 0:
                                tv = op("tensor", fn, "sT")
                            elif st == nst - 1 and dk == 7:
                                w2lo_free[0] = op("tensor", fn, "sT")
                            else:
                                op("tensor", fn)
                        if h == 1:
                            op("tensor", lambda E, pi=pi, cap=cap: E.matmul(
                                P[pi][:, 0:512], ones_bf[0:1, 0:128],
                                rb2row(cap)[0:1, 0:512], start=False, stop=True))
                            tv = op("tensor", lambda E, pi=pi, cap=cap: E.matmul(
                                P[pi][:, 512:1024], ones_bf[0:1, 0:128],
                                rb2row(cap)[0:1, 512:1024], start=False, stop=True), "sT")
                            b2row_free[0] = tv
                        wait("vector", "sT", tv)
                        drain("vector")
                        if h == 0:
                            vv = op("vector", lambda E, st=st, pi=pi: E.tensor_copy(
                                yacc_t(st), P[pi][:]), "sV")
                        else:
                            vv = op("vector", lambda E, st=st, pi=pi: E.tensor_tensor(
                                yacc_t(st), yacc_t(st), P[pi][:], op=ALU.add), "sV")
                            ysem = "sYGa" if st % 2 == 0 else "sYGb"
                            fs, fv = yst_free[st % 2]
                            wait("vector", fs, fv)
                            drain("vector")
                            cv = op("vector", lambda E, st=st, cap=cap: E.tensor_copy(
                                ryst(st % 2, cap), yacc_t(st)), "sV")
                            wait("sync", "sV", cv)
                            gv = inc(ysem, 16)
                            ops["sync"].append(lambda E, e=e, st=st, ys=ysem, cap=cap: E.dma_start(
                                out=yg[e * cap + st * 128: e * cap + (st + 1) * 128, :],
                                in_=ryst(st % 2, cap)).then_inc(SEM[ys], 16))
                            yst_free[st % 2] = (ysem, gv)
                        bank_release([pbank(pi, 0), pbank(pi, 1)], "sV", vv)
            ya, yb = cnt["sYGa"], cnt["sYGb"]
            stt_done = [0, 0]

            def emit_gather(i, t):
                wait("gpsimd", "sYGa", ya)
                wait("gpsimd", "sYGb", yb)
                sl = i % 2
                gsem = "sGB" if sl == 0 else "sGB2"
                wait("gpsimd", "sV", stt_done[sl])
                gv = op("gpsimd", lambda E, i=i, cap=cap, sl=sl: E.indirect_dma_start(
                    out=rge_v(cap, sl), out_offset=None, in_=yg,
                    in_offset=bass.IndirectOffsetOnAxis(ap=dint[:, i:i + 1], axis=0)), gsem, 16)
                wait("vector", gsem, gv)
                sv = op("vector", lambda E, t=t, i=i, cap=cap, sl=sl: E.scalar_tensor_tensor(
                    x_t(t), rge_v(cap, sl), Ssel_sb[:, i:i + 1], x_t(t),
                    op0=ALU.mult, op1=ALU.add), "sV")
                stt_done[sl] = sv

            if defer_gather:
                return emit_gather
            for i, t in enumerate(tiles):
                emit_gather(i, t)

        def attn_phase(c, tag, gb=None):
            nonlocal q_loads
            nt, own0 = CHUNKS[c]
            ln_tiles = list(range(nt)) if c == 0 else list(range(8))
            ln_phase(c, ln_tiles, tag + "_ln2", pre_tile=gb)
            barrier()
            if c > 0:
                wait("sync", "sT", cnt["sT"])
                wait("sync", "sA", cnt["sA"])
                wait("sync", "sV", cnt["sV"])
                kv_in = inc("sKV", 32)
                ops["sync"].append(lambda E: E.dma_start(
                    out=bigA.ap().rearrange("p (k c) -> p k c", c=CW)[:, 0:8, 0:256],
                    in_=kvt[:, 0:2048].rearrange("p (k c) -> p k c", c=256)).then_inc(SEM["sKV"], 16))
                ops["sync"].append(lambda E: E.dma_start(
                    out=bigA[:, V_OFF:V_OFF + 2 * D],
                    in_=kvt[:, 2048:2048 + 2 * D]).then_inc(SEM["sKV"], 16))
                wait("tensor", "sKV", kv_in)
            q_src0 = own0 * 128
            if c == 0:
                k_srcs = [(0, 512), (512, 512), (1024, 256)]
                k_dst0 = 0
            else:
                k_srcs = [(0, 512), (512, 512)]
                k_dst0 = 256
            for b in range(6):
                wait("gpsimd", "sT", cnt["sT"])
                if b == 4:
                    q_loads = inc("sQ", 16)
                    ops["gpsimd"].append(lambda E: E.dma_start(
                        out=bv_row, in_=bqkv[None, 2 * D:3 * D]).then_inc(SEM["sQ"], 16))
                for fs, fv in wq_wb_tail:
                    wait("gpsimd", fs, fv)
                wq_wb_tail.clear()
                for fs, fv in w2_wb_tail:
                    wait("gpsimd", fs, fv)
                q_loads = inc("sQ", 16)
                if c == 0:
                    ops["gpsimd"].append(lambda E, b=b: E.dma_start(
                        out=w2h.ap()[:, 8192:12288].rearrange("p (k n) -> p k n", k=KT),
                        in_=wqkv.rearrange("(k p) n -> p k n", p=128)[:, :, b * 512:(b + 1) * 512]
                    ).then_inc(SEM["sQ"], 16))
                    wait("sync", "sQ", q_loads)
                    wbv = inc("sWBq", 16)
                    ops["sync"].append(lambda E, b=b: E.dma_start(
                        out=wqs[b], in_=w2h[:, 8192:12288]).then_inc(SEM["sWBq"], 16))
                    wq_wb_tail.append(("sWBq", wbv))
                else:
                    ops["gpsimd"].append(lambda E, b=b: E.dma_start(
                        out=w2h[:, 8192:12288], in_=wqs[b]).then_inc(SEM["sQ"], 16))
                wait("tensor", "sQ", q_loads)
                if b < 2:
                    for dql in range(4):
                        dq = 4 * b + dql
                        pi = dq % 2
                        bank_wait("tensor", [pbank(pi, 0), pbank(pi, 1)])
                        for k in range(KT):
                            for g in range(2):
                                fn = lambda E, pi=pi, g=g, k=k, dql=dql, qs=q_src0: E.matmul(
                                    P[pi][:, g * 512:(g + 1) * 512],
                                    wqb_k(k, dql * 128, dql * 128 + 128),
                                    xnT_k(k, qs + g * 512, qs + (g + 1) * 512),
                                    start=(k == 0), stop=(k == KT - 1))
                                if k == KT - 1 and g == 1:
                                    tv = op("tensor", fn, "sT")
                                else:
                                    op("tensor", fn)
                        wait("scalar", "sT", tv)
                        av = op("scalar", lambda E, dq=dq, pi=pi: E.activation(
                            qT_k(dq, 0, 1024), P[pi][:], AF.Identity,
                            bias=bqkvT[:, dq:dq + 1], scale=INV_SCALE), "sA")
                        bank_release([pbank(pi, 0), pbank(pi, 1)], "sA", av)
                elif b < 4:
                    for dkl in range(4):
                        dk = 4 * (b - 2) + dkl
                        for gi, (sc, sw) in enumerate(k_srcs):
                            ps = (dk * len(k_srcs) + gi) % 4
                            pt, pcol = ps // 2, (ps % 2) * 512
                            bank_wait("tensor", [pbank(pt, ps % 2)])
                            for k in range(KT):
                                fn = lambda E, pt=pt, pcol=pcol, sw=sw, k=k, dkl=dkl, sc=sc: E.matmul(
                                    P[pt][:, pcol:pcol + sw],
                                    wqb_k(k, dkl * 128, dkl * 128 + 128),
                                    xnT_k(k, sc, sc + sw),
                                    start=(k == 0), stop=(k == KT - 1))
                                if k == KT - 1:
                                    tv = op("tensor", fn, "sT")
                                else:
                                    op("tensor", fn)
                            wait("scalar", "sT", tv)
                            av = op("scalar", lambda E, dk=dk, pt=pt, pcol=pcol, sw=sw,
                                    d0=k_dst0 + sc: E.activation(
                                kT_k(dk, d0, d0 + sw), P[pt][:, pcol:pcol + sw], AF.Identity,
                                bias=bqkvT[:, 8 + dk:9 + dk], scale=1.0), "sA")
                            bank_release([pbank(pt, ps % 2)], "sA", av)
                else:
                    vtiles = list(range(nt)) if c == 0 else list(range(2, 10))
                    for i, vt in enumerate(vtiles):
                        ps = i % 4
                        pt, pcol = ps // 2, (ps % 2) * 512
                        bank_wait("tensor", [pbank(pt, ps % 2)])
                        src_c = vt * 128 if c == 0 else (vt - 2) * 128
                        for k in range(KT):
                            op("tensor", lambda E, pt=pt, pcol=pcol, k=k, sc=src_c: E.matmul(
                                P[pt][:, pcol:pcol + 512],
                                xnT_k(k, sc, sc + 128),
                                wqb_k(k, 0, 512), start=(k == 0), stop=False))
                        tv = op("tensor", lambda E, pt=pt, pcol=pcol, b=b: E.matmul(
                            P[pt][:, pcol:pcol + 512], ones_bf[0:1, 0:128],
                            bv_row[0:1, (b - 4) * 512:(b - 3) * 512],
                            start=False, stop=True), "sT")
                        wait("vector", "sT", tv)
                        vv = op("vector", lambda E, vt=vt, b=b, pt=pt, pcol=pcol: E.tensor_copy(
                            v_t(vt, (b - 4) * 512, (b - 3) * 512), P[pt][:, pcol:pcol + 512]), "sV")
                        bank_release([pbank(pt, ps % 2)], "sV", vv)
            if c < n_chunks - 1:
                wait("sync", "sA", cnt["sA"])
                wait("sync", "sV", cnt["sV"])
                kv_out = inc("sKV", 32)
                ops["sync"].append(lambda E: E.dma_start(
                    out=kvt[:, 0:2048].rearrange("p (k c) -> p k c", c=256),
                    in_=bigA.ap().rearrange("p (k c) -> p k c", c=CW)[:, 0:8, 1024:1280]).then_inc(SEM["sKV"], 16))
                ops["sync"].append(lambda E: E.dma_start(
                    out=kvt[:, 2048:2048 + 2 * D],
                    in_=bigA[:, V_OFF + 8 * D:V_OFF + 10 * D]).then_inc(SEM["sKV"], 16))
                wait("scalar", "sKV", kv_out)
                wait("tensor", "sKV", kv_out)
                wait("vector", "sKV", kv_out)
            spare = [0, 1] if c == 0 else [8, 9]
            def emit_scores(qt):
                sslot = qt % 2
                bank_wait("tensor", [4 + sslot])
                for kj in range(3):
                    for dq in range(KT):
                        fn = lambda E, sslot=sslot, kj=kj, dq=dq, qt=qt: E.matmul(
                            P[2][:, sslot * 512 + kj * 128: sslot * 512 + kj * 128 + 128],
                            kT_k(dq, (qt + kj) * 128, (qt + kj + 1) * 128),
                            qT_k(dq, qt * 128, (qt + 1) * 128),
                            start=(dq == 0), stop=(dq == KT - 1))
                        if dq == KT - 1:
                            tv = op("tensor", fn, "sT")
                        else:
                            op("tensor", fn)
                wait("vector", "sT", tv)
                if qt - 2 in av_done:
                    wait("vector", "sT", av_done[qt - 2])
                MKCOL = {MASK_MU: 0, MASK_ML: 1, MASK_A: 2, MASK_B: 3, MASK_C: 4}
                for kj in range(3):
                    if qt == 0 and c == 0:
                        midx = [MASK_A, MASK_B, MASK_MU][kj]
                    elif qt == 1 and c == 0:
                        midx = [MASK_C, MASK_ONES, MASK_MU][kj]
                    else:
                        midx = [MASK_ML, MASK_ONES, MASK_MU][kj]
                    if midx == MASK_ONES:
                        mv = op("vector", lambda E, sslot=sslot, kj=kj: E.tensor_copy(
                            scob_s(sslot, kj * 128, (kj + 1) * 128),
                            P[2][:, sslot * 512 + kj * 128: sslot * 512 + kj * 128 + 128]), "sV")
                    else:
                        mc = MKCOL[midx]
                        mv = op("vector", lambda E, sslot=sslot, kj=kj, mc=mc: E.tensor_tensor(
                            scob_s(sslot, kj * 128, (kj + 1) * 128),
                            P[2][:, sslot * 512 + kj * 128: sslot * 512 + kj * 128 + 128],
                            mk_sb[:, mc * 128:(mc + 1) * 128], op=ALU.mult), "sV")
                bank_release([4 + sslot], "sV", mv)
                mask_done[qt] = mv

            def emit_av_ln(qt):
                xq = own0 + qt
                sslot = qt % 2
                pi = 3 if qt % 2 == 0 else 0
                bank_wait("tensor", [pbank(pi, 0), pbank(pi, 1)])
                wait("tensor", "sV", mask_done[qt])
                for kj in range(3):
                    for g in range(2):
                        fn = lambda E, pi=pi, g=g, sslot=sslot, kj=kj, qt=qt: E.matmul(
                            P[pi][:, g * 512:(g + 1) * 512],
                            scob_s(sslot, kj * 128, (kj + 1) * 128),
                            v_t(qt + kj, g * 512, (g + 1) * 512),
                            start=(kj == 0), stop=(kj == 2))
                        if kj == 2 and g == 1:
                            tv = op("tensor", fn, "sT")
                        else:
                            op("tensor", fn)
                av_done[qt] = tv
                wait("vector", "sT", tv)
                att = yacc_t(spare[0])
                sq = yacc_t(spare[1])
                s = (qt % 2) * 32
                avv = op("vector", lambda E, att=att, pi=pi: E.tensor_copy(att, P[pi][:]), "sV")
                bank_release([pbank(pi, 0), pbank(pi, 1)], "sV", avv)
                drain("vector")
                op("vector", lambda E, att=att, s=s: E.reduce_sum(
                    stats[:, s:s + 1], att, axis=AX.X))
                wait("scalar", "sV", avv)
                av = op("scalar", lambda E, att=att, sq=sq: E.activation(
                    sq, att, AF.Square), "sA")
                wait("vector", "sA", av)
                op("vector", lambda E, sq=sq, s=s: E.reduce_sum(
                    stats[:, s + 1:s + 2], sq, axis=AX.X))
                drain("vector")
                op("vector", lambda E, s=s: E.tensor_scalar(
                    stats[:, s + 2:s + 3], stats[:, s:s + 1], 1.0 / D, None, op0=ALU.mult))
                drain("vector")
                vv = op("vector", lambda E, s=s: E.scalar_tensor_tensor(
                    stats[:, s + 3:s + 4], stats[:, s:s + 1], stats[:, s + 2:s + 3],
                    stats[:, s + 1:s + 2], op0=ALU.mult, op1=ALU.subtract), "sV")
                wait("scalar", "sV", vv)
                av = op("scalar", lambda E, s=s: E.activation(
                    stats[:, s + 4:s + 5], stats[:, s + 3:s + 4], AF.Sqrt,
                    bias=stats[:, 64:65], scale=-1.0 / D), "sA")
                wait("vector", "sA", av)
                op("vector", lambda E, s=s: E.reciprocal(
                    stats[:, s + 5:s + 6], stats[:, s + 4:s + 5]))
                drain("vector")
                op("vector", lambda E, att=att, sq=sq, s=s: E.tensor_scalar(
                    sq, att, stats[:, s + 2:s + 3], stats[:, s + 5:s + 6],
                    op0=ALU.subtract, op1=ALU.mult))
                drain("vector")
                op("vector", lambda E, sq=sq: E.tensor_tensor(sq, sq, lnwb[:], op=ALU.mult))
                drain("vector")
                op("vector", lambda E, sq=sq: E.tensor_tensor(sq, sq, lnbb[:], op=ALU.add))
                drain("vector")
                op("vector", lambda E, sq=sq, xq=xq: E.tensor_tensor(
                    x_t(xq), x_t(xq), sq, op=ALU.add), "sV")

            mask_done = {}
            av_done = {}
            for qt in range(8):
                emit_scores(qt)
                if qt >= 1:
                    emit_av_ln(qt - 1)
            emit_av_ln(7)
        for c in range(n_chunks):
            nt, own0 = CHUNKS[c]
            all_tiles = list(range(nt))
            own_tiles = list(range(own0, own0 + 8))
            barrier()
            cap1 = 512 if c == 0 else CAP
            ws1 = "f32wb" if c == 0 else "bf16"
            pre1 = prefetch_e0(c) if ws1 == "bf16" else None
            lnr_phase(c, all_tiles, f"c{c}r1", cap=cap1,
                      xw=(list(x_marks) if c > 0 else [16] * 5))
            barrier()
            gb1 = routed_ffn_phase(c, all_tiles, f"c{c}f1", cap=cap1, wsrc=ws1,
                                   pre=pre1, defer_gather=do_attn)
            if do_attn:
                attn_phase(c, f"c{c}a", gb=gb1)
            if do_ffn2:
                barrier()
                pre2 = prefetch_e0(c)
                lnr_phase(c, own_tiles, f"c{c}r2", cap=CAP)
                barrier()
                routed_ffn_phase(c, own_tiles, f"c{c}f2", cap=CAP, wsrc="bf16", pre=pre2)
            wait("sync", "sV", cnt["sV"])
            stores += 1
            op("sync", lambda E, own0=own0, c=c: E.dma_start(
                out=out[c * 1024:(c + 1) * 1024, :].rearrange("(t p) n -> p t n", p=128),
                in_=x_sb.ap().rearrange("p (t n) -> p t n", t=NT_MAX)[:, own0:own0 + 8, :]),
                "sOut", 16)
            if c + 1 < n_chunks:
                wait("sync", "sOut", 16 * stores)
                g0 = (10 + 8 * c) * 128
                x_marks.clear()
                for part in range(4):
                    wait("sync", "sX", 16 * x_loads)
                    x_loads += 1
                    v = op("sync", lambda E, g0=g0, part=part: E.dma_start(
                        out=x_sb.ap().rearrange("p (t n) -> p t n", t=NT_MAX)[:, 2 * part:2 * part + 2, :],
                        in_=x_in[g0 + 256 * part:g0 + 256 * (part + 1), :].rearrange(
                            "(t p) n -> p t n", p=128)), "sX", 16)
                    x_marks.append(v)
        wait("sync", "sOut", 16 * stores)

        with nc.Block() as block:
            @block.tensor
            def _(E):
                for fn in ops["tensor"]:
                    fn(E)

            @block.vector
            def _(E):
                for fn in ops["vector"]:
                    fn(E)

            @block.scalar
            def _(E):
                for fn in ops["scalar"]:
                    fn(E)

            @block.gpsimd
            def _(E):
                for fn in ops["gpsimd"]:
                    fn(E)

            @block.sync
            def _(E):
                for fn in ops["sync"]:
                    fn(E)

    return nc


def make_masks(h):
    MU = np.triu(np.ones((128, 128), dtype=np.float32))
    ML = np.tril(np.ones((128, 128), dtype=np.float32), -1)
    ONES = np.ones((128, 128), dtype=np.float32)
    Z = np.zeros((128, 128), dtype=np.float32)
    if h == 1:
        MA, MB, MC = ML, ONES, ML
    else:
        MA, MB, MC = Z, Z, Z
    return np.stack([MU, ML, MA, MB, MC])


def kernel(L0_attn_x, L0_attn_W_qkv, L0_attn_b_qkv, L0_attn_ln_w, L0_attn_ln_b,
           L0_ffn_Wr, L0_ffn_br, L0_ffn_W1, L0_ffn_b1, L0_ffn_W2, L0_ffn_b2):
    x = np.ascontiguousarray(np.asarray(L0_attn_x, dtype=np.float32))
    wqkv = np.ascontiguousarray(np.asarray(L0_attn_W_qkv, dtype=np.float32))
    bqkv = np.ascontiguousarray(np.asarray(L0_attn_b_qkv, dtype=np.float32))
    lnw = np.ascontiguousarray(np.asarray(L0_attn_ln_w, dtype=np.float32))
    lnb = np.ascontiguousarray(np.asarray(L0_attn_ln_b, dtype=np.float32))
    wr = np.ascontiguousarray(np.asarray(L0_ffn_Wr, dtype=np.float32))
    br = np.ascontiguousarray(np.asarray(L0_ffn_br, dtype=np.float32))
    w1 = np.ascontiguousarray(np.asarray(L0_ffn_W1, dtype=np.float32))
    b1 = np.ascontiguousarray(np.asarray(L0_ffn_b1, dtype=np.float32))
    w2 = np.ascontiguousarray(np.asarray(L0_ffn_W2, dtype=np.float32))
    b2 = np.ascontiguousarray(np.asarray(L0_ffn_b2, dtype=np.float32))

    B, T, _ = x.shape
    idf = np.eye(128, dtype=np.float32)
    triv = np.zeros((128, 136), dtype=ml_dtypes.bfloat16)
    triv[:, 0:128] = np.triu(np.ones((128, 128), np.float32)).astype(ml_dtypes.bfloat16)
    triv[:, 128] = 1
    ecapv = np.stack([np.arange(8, dtype=np.float32) * 512,
                      np.arange(8, dtype=np.float32) * 256], axis=1)
    ones = np.ones((1, 512), dtype=np.float32)

    in_maps = []
    for c in range(8):
        b, h = c // 2, c % 2
        start = h * OWN
        if h == 0:
            xc = np.concatenate([np.zeros((HALO, D), np.float32),
                                 x[b, 0:OWN]], axis=0)
        else:
            xc = x[b, start - HALO:start + OWN]
        in_maps.append({
            "x_in": np.ascontiguousarray(xc),
            "w1": w1, "w2": w2, "wqkv": wqkv, "wr": wr, "br": br,
            "b1": b1, "b2": b2, "bqkv": bqkv, "lnw": lnw, "lnb": lnb,
            "idf": idf, "masks": make_masks(h), "ones_in": ones,
            "tri": triv, "ecap": ecapv,
        })
    nc = build_program()
    res = run_bass_kernel_spmd(nc, in_maps, core_ids=list(range(8)))
    outp = np.empty((B, T, D), dtype=np.float32)
    for c in range(8):
        b, h = c // 2, c % 2
        outp[b, h * OWN:(h + 1) * OWN] = res.results[c]["out"]
    return outp


if __name__ == "__main__":
    nc = build_program()
    n = sum(1 for _ in nc.all_instructions())
    print("built ok, instructions:", n)
